# version 17
# speedup vs baseline: 1.2039x; 1.0026x over previous
import os
import numpy as np
import contextlib

import concourse.bass as bass
import concourse.mybir as mybir
from concourse.bass_utils import run_bass_kernel_spmd

F32 = mybir.dt.float32
BF16 = mybir.dt.bfloat16
I32 = mybir.dt.int32
AF = mybir.ActivationFunctionType
ALU = mybir.AluOpType
AX = mybir.AxisListType

L, DM, DI, H, HD, C = 2048, 1024, 2048, 16, 128, 128
NT = L // 128
NG = 4
GT = 512
NKD = DM // 128
NKI = DI // 128
PI = float(np.pi)

DEBUG = False


def make_consts():
    ident = np.eye(128, dtype=np.float32)
    tt = np.arange(128)
    maskS = np.where(tt[:, None] >= tt[None, :], -1e30, 0.0)
    maskI = np.where(tt[:, None] > tt[None, :], -1e30, 0.0)
    cmask = np.concatenate([maskS, maskI], axis=1).astype(np.float32)
    sel = np.zeros((16, 16 * 128), dtype=np.float32)
    for h in range(16):
        sel[h, 128 * h:128 * (h + 1)] = 1.0
    sel8 = np.zeros((8, 8 * 128), dtype=np.float32)
    for r in range(8):
        sel8[r, 128 * r:128 * (r + 1)] = 1.0
    sel32 = np.zeros((32, 32 * 128), dtype=np.float32)
    for r in range(32):
        sel32[r, 128 * r:128 * (r + 1)] = 1.0
    freqs = (1.0 / 10000.0 ** (np.arange(16, dtype=np.float64) / 16.0)).astype(np.float32)
    return dict(
        c_ident=ident,
        c_cmask=cmask,
        c_sel=sel,
        c_sel8=sel8,
        c_sel32=sel32,
        c_ones=np.ones((128, 512), dtype=np.float32),
        c_freqs=freqs.reshape(1, 16).copy(),
    )


class Prog:
    ENGINES = ("sync", "tensor", "vector", "scalar", "gpsimd")

    def __init__(self):
        self.ops = {e: [] for e in self.ENGINES}

    def run(self, block):
        for e in self.ENGINES:
            ops = self.ops[e]
            deco = getattr(block, e)

            def make(ops):
                def body(eng):
                    for fn in ops:
                        fn(eng)
                return body
            deco(make(ops))


class Ctr:
    def __init__(self, handle):
        self.h = handle
        self.n = 0


def nc_semaphore(nc, es, name):
    return nc.semaphore("s" + name)


class B:

    def __init__(self, nc, es):
        self.nc = nc
        self.es = es
        self.prog = Prog()
        self.sem = {}
        for name in ("PE", "VE", "SE", "GE"):
            self.sem[name] = Ctr(es.enter_context(nc.semaphore("s" + name)))

    def ctr(self, name):
        if name not in self.sem:
            self.sem[name] = Ctr(self.es.enter_context(nc_semaphore(self.nc, self.es, name)))
        return self.sem[name]

    def op(self, engine, fn, waits=(), inc=None, inck=1):
        waits = [(c, v) for (c, v) in waits if v > 0]
        if inc is not None:
            inc.n += inck
        sem = inc
        k = inck

        def run(eng):
            for (c, v) in waits:
                eng.wait_ge(c.h, v)
            instr = fn(eng)
            if sem is not None:
                instr.then_inc(sem.h, k)
        self.prog.ops[engine].append(run)
        return inc.n if inc is not None else None

    def barrier(self):
        vals = [(c, c.n) for c in self.sem.values() if c.n > 0]
        for eng in Prog.ENGINES:
            for (c, v) in vals:
                self.wait(eng, c, v)

    def wait(self, engine, ctr, value):
        if value <= 0:
            return
        def run(eng):
            eng.wait_ge(ctr.h, value)
        self.prog.ops[engine].append(run)

    def dma(self, queue, out, in_, waits=(), inc=None):
        return self.op(queue, lambda e: e.dma_start(out=out, in_=in_),
                       waits=waits, inc=inc, inck=16)


def build_nc(debug=False):
    nc = bass.Bass("TRN2", target_bir_lowering=False, debug=False,
                   detect_race_conditions=False)
    dram = {}

    def din(name, shape, dtype=F32):
        dram[name] = nc.dram_tensor(name, list(shape), dtype, kind="ExternalInput").ap()

    def dout(name, shape, dtype=F32):
        dram[name] = nc.dram_tensor(name, list(shape), dtype, kind="ExternalOutput").ap()

    def dint(name, shape, dtype):
        dram[name] = nc.dram_tensor(name, list(shape), dtype, kind="Internal").ap()

    din("x", (L, DM))
    din("inw", (DM, 6176))
    din("q_w", (DI, DI))
    din("readout_w", (32, DI))
    din("out_w", (DI, DM))
    din("fc1_q", (DM, 4096), I32)
    din("fc2_q", (4096, DM), I32)
    din("fc1_s", (8, 4096))
    din("fc2_s", (32, DM))
    din("gate_w_t", (128, NKI * 112))
    din("rms_w_c", (128, NKD))
    din("inb_zx_c", (128, 32))
    din("inb_kv_r", (1, 2080))
    din("conv_w_c", (128, NKI * 4))
    din("conv_b_c", (128, NKI))
    din("gate_b_r", (1, 112))
    din("lds_r", (1, H))
    din("gn_w_c", (128, NKI))
    din("gn_b_c", (128, NKI))
    din("d_param_c", (128, NKI))
    din("ln_g_c", (128, NKD))
    din("ln_b_c", (128, NKD))
    din("fc1_b_c", (128, 32))
    din("fc2_b_r", (1, DM))
    for k, v in make_consts().items():
        din(k, v.shape)
    dout("out", (L, DM))

    dint("sp_xn", (L, DM), F32)
    dint("sp_z", (NKI, 128, L), BF16)
    dint("sp_xc", (NKI, 128, L), BF16)
    dint("sp_k", (L, DI), BF16)
    dint("sp_s", (L, DM), F32)
    dint("sp_y1", (32, 128, L), BF16)

    if debug:
        dout("dbg_xn", (L, DM))
        dout("dbg_z", (NKI, 128, L), BF16)
        dout("dbg_xc", (NKI, 128, L), BF16)
        dout("dbg_k", (L, DI), BF16)
        dout("dbg_gates", (NT, 128, 112))
        dout("dbg_n2k", (NT, 128, H))
        dout("dbg_retr", (L, 32))
        dout("dbg2_gsc", (2, 128, 448))
        dout("dbg2_tmb", (2, 128, 80))
        dout("dbg2_cols", (2, 128, 16 * 9))
        dout("dbg2_expo", (2, 128, 256), BF16)
        dout("dbg2_bts", (2, 128, 2048), BF16)
        dout("dbg2_bkqs", (2, 128, 2048), BF16)
        dout("dbg2_uhat", (2, 128, 32), BF16)
        dout("dbg2_rtlb", (2, 128, 32), BF16)
        dout("dbg2_kchT", (2, 128, 128), BF16)
        dout("dbg2_n2q", (2, 128, 16))
        dout("dbg3_b1c", (128, 32))
        dout("dbg3_w1", (128, 4096), BF16)
        dout("dbg3_u2T", (128, 2048), BF16)
        dout("dbg3_rstd2", (128, 16))
        dout("dbg2_vg", (2, 128, 32))
        dout("dbg2_vres", (2, 128, 32))
        dout("dbg2_rtl2", (2, 128, 32), BF16)
        dout("dbg2_eiv", (2, 128, 32))
        dout("dbg2_a32", (2, 128, 32))
        dout("dbg2_b32", (2, 128, 32))
        dout("dbg2_c32", (2, 128, 32))
        dout("dbg2_t1", (2, 128, 32))
        dout("dbg2_sph0", (2, 128, 16))
        dout("dbg2_c32a", (2, 128, 32))
        dout("dbg_s", (L, DM))
        dout("dbg_g", (L, DM))

    es = contextlib.ExitStack()
    with es:
        b = B(nc, es)
        build_program(nc, b, es, dram, debug)
        block = es.enter_context(nc.Block())
        b.prog.run(block)
    return nc


def build_program(nc, b, es, dram, debug):
    KSTOP = int(os.environ.get("KSTOP", "99"))
    KSUB = int(os.environ.get("KSUB", "99"))
    PE, VE, SE, GE = (b.sem[k] for k in ("PE", "VE", "SE", "GE"))
    S0 = b.ctr("S0")
    G0 = b.ctr("G0")
    XL = [b.ctr("XL0"), b.ctr("XL1")]
    XNS = b.ctr("XNS")
    KS = b.ctr("KS")
    ZS = b.ctr("ZS")
    XCS = b.ctr("XCS")
    DBG = b.ctr("DBG")
    OUT = b.ctr("OUT")

    def sb(name, shape, dtype=F32):
        return es.enter_context(nc.sbuf_tensor(name, list(shape), dtype))

    cI = sb("cI", (128, 128))
    cIb = sb("cIb", (128, 128), BF16)
    cMask = sb("cMask", (128, 256), BF16)
    cOnes = sb("cOnes", (128, 512))
    cFreqs = sb("cFreqs", (128, 16))
    cEdts = sb("cEdts", (128, 16))
    cpool = sb("cpool", (128, 4))

    w_inb_zx = sb("w_inb_zx", (128, 32))
    w_convw = sb("w_convw", (128, NKI * 4))
    w_convb = sb("w_convb", (128, NKI))
    w_gatew = sb("w_gatew", (128, NKI * 112), BF16)
    w_gateb = sb("w_gateb", (1, 112))
    w_lds = sb("w_lds", (1, H))
    w_rms = sb("w_rms", (128, NKD))
    w_gnw = sb("w_gnw", (128, NKI))
    w_gnb = sb("w_gnb", (128, NKI))
    w_dpar = sb("w_dpar", (128, NKI))

    arena = sb("arena", (128, 49664), BF16)
    IN0 = 0

    vres = sb("vres", (128, NT * 32))
    gres = sb("gres", (128, NT * 112))
    n2k = sb("n2k", (128, NT * H))
    rstd2c = sb("rstd2c", (128, NT))
    retr = sb("retr", (128, NT * 32))
    st_re = sb("st_re", (128, H * 2))

    pb = [es.enter_context(nc.psum_tensor(f"pb{i}", [128, 512], F32))
          for i in range(8)]

    nc.const_aps.register(F32, 0.0, cpool[:, 0:1])
    nc.const_aps.register(F32, 1.0, cpool[:, 1:2])
    nc.const_aps.register(F32, 1e-5, cpool[:, 2:3])
    nc.const_aps.register(F32, 1e-6, cpool[:, 3:4])

    b.op("vector", lambda v: v.memset(cpool[:, 0:1], 0.0), inc=VE)
    b.op("vector", lambda v: v.memset(cpool[:, 1:2], 1.0), inc=VE)
    b.op("vector", lambda v: v.memset(cpool[:, 2:3], 1e-5), inc=VE)
    b.op("vector", lambda v: v.memset(cpool[:, 3:4], 1e-6), inc=VE)
    b.op("vector", lambda v: v.memset(st_re[:], 0.0), inc=VE)
    ve_pool = VE.n

    b.dma("sync", cI[:], dram["c_ident"], inc=S0)
    b.dma("gpsimd", cIb[:], dram["c_ident"], inc=G0)
    b.dma("gpsimd", cMask[:], dram["c_cmask"], inc=G0)
    b.dma("sync", cOnes[:], dram["c_ones"], inc=S0)
    b.dma("sync", w_rms[:], dram["rms_w_c"], inc=S0)
    b.dma("sync", w_inb_zx[:], dram["inb_zx_c"], inc=S0)
    b.dma("sync", w_convw[:], dram["conv_w_c"], inc=S0)
    b.dma("sync", w_convb[:], dram["conv_b_c"], inc=S0)
    b.dma("gpsimd", w_gatew[:], dram["gate_w_t"], inc=G0)
    b.dma("sync", w_gateb[:], dram["gate_b_r"], inc=S0)
    b.dma("sync", w_lds[:], dram["lds_r"], inc=S0)
    b.dma("sync", w_gnw[:], dram["gn_w_c"], inc=S0)
    b.dma("sync", w_gnb[:], dram["gn_b_c"], inc=S0)
    b.dma("sync", w_dpar[:], dram["d_param_c"], inc=S0)
    b.dma("sync", cFreqs[0:1, :], dram["c_freqs"], inc=S0)
    p0_small = S0.n

    for r in range(NKD):
        b.dma("gpsimd", arena[:, IN0 + r * 6176: IN0 + (r + 1) * 6176],
              dram["inw"][128 * r:128 * (r + 1), :], inc=G0)
    p0_inw = G0.n

    b.op("tensor", lambda t: t.matmul(pb[7][:, 0:16], cOnes[0:1, 0:128],
                                      cFreqs[0:1, 0:16], start=True, stop=True),
         waits=[(S0, p0_small)], inc=PE)
    b.op("vector", lambda v: v.tensor_copy(cFreqs[:], pb[7][:, 0:16]),
         waits=[(PE, PE.n)], inc=VE)
    b.op("scalar", lambda s: s.activation(w_lds[:], w_lds[:], AF.Exp),
         waits=[(S0, p0_small), (VE, ve_pool)], inc=SE)
    b.op("tensor", lambda t: t.matmul(pb[7][:, 16:32], cOnes[0:1, 0:128],
                                      w_lds[0:1, :], start=True, stop=True),
         waits=[(SE, SE.n)], inc=PE)
    b.op("vector", lambda v: v.tensor_copy(cEdts[:], pb[7][:, 16:32]),
         waits=[(PE, PE.n)], inc=VE)
    for r in range(NKD):
        def f(v, r=r):
            return v.tensor_scalar(arena[:, IN0 + r * 6176: IN0 + (r + 1) * 6176],
                                   arena[:, IN0 + r * 6176: IN0 + (r + 1) * 6176],
                                   w_rms[:, r:r + 1], None, ALU.mult)
        b.op("vector", f, waits=[(G0, p0_inw)], inc=VE)
    p0_ve = VE.n
    p0_pe = PE.n

    with contextlib.ExitStack() as p1s:
        def sb1(name, shape, dtype=F32):
            return p1s.enter_context(nc.sbuf_tensor(name, list(shape), dtype))

        xx = [sb1("xx0", (128, DM)), sb1("xx1", (128, DM))]
        w_inb_kv = sb1("w_inb_kv", (1, 2080))
        S1 = b.ctr("S1")
        b.dma("sync", w_inb_kv[:], dram["inb_kv_r"], inc=S1)
        colm = sb1("colm", (128, 8))
        xn_tm = sb1("xn_tm", (128, DM))
        xnr_b = sb1("xnr_b", (128, DM), BF16)
        xnrT = sb1("xnrT", (128, NKD * GT), BF16)
        zev = sb1("zev", (128, GT), BF16)
        xg = sb1("xg", (128, GT + 3), BF16)
        xghalo = sb1("xghalo", (128, NKI * 4), BF16)
        cvy = sb1("cvy", (128, GT), BF16)
        xcv = sb1("xcv", (128, GT), BF16)
        kev = sb1("kev", (128, 2048), BF16)
        ksq = sb1("ksq", (128, 1024))

        memset_ve = b.op("vector", lambda v: v.memset(xghalo[:], 0.0), inc=VE)
        halo_out_ve = [memset_ve] * NKI
        xg_free_se = 0

        colm_free_se = 0
        rstd2_free_se = 0
        xn_free_st = [0]
        xbuf_free_se = [0, 0]
        xnr_free_pe = 0
        xnrT_free_pe = p0_pe
        kev_free_st = 0
        zev_free_st = 0
        xcv_free_st = 0
        xcv_free_pe = 0
        cvy_free_se = 0
        ksq_tile_se = 0
        zx_evac_se = {}
        gates_started = False

        for g in range(NG if KSTOP >= 5 else (1 if KSTOP >= 1 else 0)):
            for u in range(4):
                ti = g * 4 + u
                xt = xx[ti % 2]
                ldv = b.dma("sync", xt[:], dram["x"][128 * ti:128 * (ti + 1), :],
                            waits=[(SE, xbuf_free_se[ti % 2])], inc=XL[ti % 2])
                sq_se = b.op("scalar", lambda s, xt=xt: s.activation(
                    ksq[:, 0:DM], xt[:], AF.Square),
                     waits=[(XL[ti % 2], ldv), (VE, ve_pool)], inc=SE)
                ra = b.op("vector", lambda v, xt=xt: v.reduce_sum(
                    colm[:, 0:1], xt[:], AX.X),
                     waits=[(XL[ti % 2], ldv), (SE, colm_free_se)], inc=VE)
                rb = b.op("vector", lambda v: v.reduce_sum(
                    colm[:, 1:2], ksq[:, 0:DM], AX.X),
                     waits=[(SE, sq_se)], inc=VE)
                rc = b.op("vector", lambda v: v.scalar_tensor_tensor(
                    colm[:, 2:3], colm[:, 0:1], 1.0 / (DM * DM), colm[:, 0:1],
                    ALU.mult, ALU.mult), waits=[(VE, ra)], inc=VE)
                rd = b.op("vector", lambda v: v.scalar_tensor_tensor(
                    colm[:, 3:4], colm[:, 1:2], 1.0 / DM, colm[:, 2:3],
                    ALU.mult, ALU.subtract), waits=[(VE, rc)], inc=VE)
                re_ = b.op("scalar", lambda s: s.activation(
                    colm[:, 4:5], colm[:, 3:4], AF.Ln, bias=1e-5),
                     waits=[(VE, rd)], inc=SE)
                rf = b.op("scalar", lambda s: s.activation(
                    colm[:, 4:5], colm[:, 4:5], AF.Exp, scale=-0.5),
                     waits=[(SE, re_)], inc=SE)
                rg = b.op("vector", lambda v: v.scalar_tensor_tensor(
                    colm[:, 7:8], colm[:, 0:1], -1.0 / DM, colm[:, 4:5],
                    ALU.mult, ALU.mult), waits=[(SE, rf)], inc=VE)
                rh = b.op("vector", lambda v: v.tensor_tensor(
                    colm[:, 5:6], colm[:, 4:5], colm[:, 4:5], ALU.mult),
                     waits=[(SE, rf)], inc=VE)
                ri = b.op("vector", lambda v: v.tensor_tensor(
                    colm[:, 5:6], colm[:, 5:6], colm[:, 3:4], ALU.mult),
                     waits=[(VE, rh)], inc=VE)
                rj = b.op("scalar", lambda s: s.activation(
                    colm[:, 6:7], colm[:, 5:6], AF.Ln, bias=1e-6),
                     waits=[(VE, ri)], inc=SE)
                rk_ = b.op("scalar", lambda s: s.activation(
                    colm[:, 6:7], colm[:, 6:7], AF.Exp, scale=-0.5),
                     waits=[(SE, rj)], inc=SE)
                rl = b.op("scalar", lambda s, ti=ti: s.activation(
                    rstd2c[:, ti:ti + 1], colm[:, 5:6], AF.Ln, bias=1e-5),
                     waits=[(VE, ri), (SE, rstd2_free_se)], inc=SE)
                b.op("scalar", lambda s, ti=ti: s.activation(
                    rstd2c[:, ti:ti + 1], rstd2c[:, ti:ti + 1], AF.Exp, scale=-0.5),
                     waits=[(SE, rl)], inc=SE)
                xn_se = b.op("scalar", lambda s, xt=xt: s.activation(
                    xn_tm[:], xt[:], AF.Identity, bias=colm[:, 7:8],
                    scale=colm[:, 4:5]),
                     waits=[(VE, rg), (XNS, xn_free_st[0])], inc=SE)
                xnr_se = b.op("scalar", lambda s: s.activation(
                    xnr_b[:], xn_tm[:], AF.Identity, scale=colm[:, 6:7]),
                     waits=[(SE, xn_se), (SE, rk_), (PE, xnr_free_pe)], inc=SE)
                colm_free_se = xnr_se
                xbuf_free_se[ti % 2] = xn_se
                xn_free_st[0] = b.dma("scalar",
                                      dram["sp_xn"][128 * ti:128 * (ti + 1), :],
                                      xn_tm[:], waits=[(SE, xn_se)], inc=XNS)
                if debug:
                    xn_free_st[0] = b.dma("scalar",
                                          dram["dbg_xn"][128 * ti:128 * (ti + 1), :],
                                          xn_tm[:], inc=XNS)
                if KSTOP < 3:
                    continue
                tev_ve = {}
                for r in range(NKD):
                    bank = 6 + (r % 2)
                    w = [(SE, xnr_se)]
                    if r < 2:
                        w.append((VE, VE.n))
                        if u == 0 and r == 0:
                            w.append((PE, xnrT_free_pe))
                            w.append((G0, p0_inw))
                    else:
                        w.append((VE, tev_ve[r - 2]))
                    def tmm(t, r=r, bank=bank):
                        return t.matmul(pb[bank][:, 0:128],
                                        xnr_b[:, 128 * r:128 * (r + 1)],
                                        cIb[:], start=True, stop=True)
                    pe_t = b.op("tensor", tmm, waits=w, inc=PE)
                    def tev(v, r=r, u=u, bank=bank):
                        return v.tensor_copy(
                            xnrT[:, r * GT + u * 128: r * GT + (u + 1) * 128],
                            pb[bank][:, 0:128])
                    tev_ve[r] = b.op("vector", tev, waits=[(PE, pe_t)], inc=VE)
                xnr_free_pe = PE.n
                ve_tr = VE.n
                if KSTOP < 4:
                    continue
                kv_w = [(VE, ve_tr), (SE, ksq_tile_se), (S1, 16)]
                first = True
                for n in range(4):
                    for r in range(NKD):
                        def kmm(t, r=r, n=n, u=u):
                            return t.matmul(
                                pb[n][:, :],
                                xnrT[:, r * GT + u * 128: r * GT + (u + 1) * 128],
                                arena[:, IN0 + r * 6176 + 4096 + n * 512:
                                      IN0 + r * 6176 + 4096 + (n + 1) * 512],
                                start=(r == 0), stop=False)
                        b.op("tensor", kmm, waits=kv_w if first else [], inc=None)
                        first = False
                    def kb(t, n=n):
                        return t.matmul(pb[n][:, :], cOnes[0:1, 0:128],
                                        w_inb_kv[:, n * 512:(n + 1) * 512],
                                        start=False, stop=True)
                    b.op("tensor", kb, inc=PE)
                for r in range(NKD):
                    def vmm(t, r=r, u=u):
                        return t.matmul(
                            pb[4][:, 0:32],
                            xnrT[:, r * GT + u * 128: r * GT + (u + 1) * 128],
                            arena[:, IN0 + r * 6176 + 6144: IN0 + r * 6176 + 6176],
                            start=(r == 0), stop=False)
                    b.op("tensor", vmm, inc=None)
                b.op("tensor", lambda t: t.matmul(pb[4][:, 0:32], cOnes[0:1, 0:128],
                                                  w_inb_kv[:, 2048:2080],
                                                  start=False, stop=True), inc=PE)
                pe_kv = PE.n
                if KSUB < 1:
                    continue
                b.wait("vector", KS, kev_free_st)
                nrm_ve_prev = 0
                for n in range(4):
                    cp_ve = b.op("vector", lambda v, n=n: v.tensor_copy(
                        kev[:, 512 * n:512 * (n + 1)], pb[n][:, :]),
                         waits=[(PE, pe_kv)], inc=VE)
                    if KSUB < 2:
                        continue
                    half = 512 * (n % 2)
                    sq_se_n = b.op("scalar", lambda s, n=n, half=half: s.activation(
                        ksq[:, half:half + 512], kev[:, 512 * n:512 * (n + 1)],
                        AF.Square),
                         waits=[(VE, cp_ve)], inc=SE)
                    def nrm(v, ti=ti, n=n, half=half):
                        srcv = ksq[:, half:half + 512].rearrange(
                            "p (h d) -> p h d", d=128)
                        return v.reduce_sum(
                            n2k[:, H * ti + 4 * n: H * ti + 4 * (n + 1)], srcv, AX.X)
                    nrm_ve_prev = b.op("vector", nrm, waits=[(SE, sq_se_n)], inc=VE)
                ksq_tile_se = SE.n
                b.op("vector", lambda v, ti=ti: v.tensor_copy(
                    vres[:, 32 * ti:32 * (ti + 1)], pb[4][:, 0:32]),
                     waits=[(PE, pe_kv)], inc=VE)
                kev_ve = VE.n
                if KSUB < 3:
                    continue
                kev_free_st = b.dma("scalar",
                                    dram["sp_k"][128 * ti:128 * (ti + 1), :], kev[:],
                                    waits=[(VE, kev_ve)], inc=KS)
                if debug:
                    kev_free_st = b.dma("scalar",
                                        dram["dbg_k"][128 * ti:128 * (ti + 1), :],
                                        kev[:], inc=KS)

            if KSTOP < 5:
                continue
            ve_xnrt_done = ve_tr
            gates_started = True
            for m in range(32):
                bank = 5 + (m % 2)
                for r in range(NKD):
                    w = []
                    if r == 0:
                        w.append((VE, ve_xnrt_done))
                        if (m - 2) in zx_evac_se:
                            w.append((SE, zx_evac_se[m - 2]))
                    def zmm(t, m=m, r=r, bank=bank):
                        return t.matmul(pb[bank][:, :],
                                        arena[:, IN0 + r * 6176 + m * 128:
                                              IN0 + r * 6176 + (m + 1) * 128],
                                        xnrT[:, r * GT:(r + 1) * GT],
                                        start=(r == 0), stop=(r == NKD - 1))
                    b.op("tensor", zmm, waits=w, inc=PE if r == NKD - 1 else None)
                pe_zx = PE.n
                if m < 16:
                    def zv(s, m=m, bank=bank):
                        return s.activation(zev[:], pb[bank][:, :], AF.Identity,
                                            bias=w_inb_zx[:, m:m + 1])
                    b.op("scalar", zv, waits=[(PE, pe_zx), (ZS, zev_free_st)], inc=SE)
                    zx_evac_se[m] = SE.n
                    zev_free_st = b.dma("scalar",
                                        dram["sp_z"][m, :, GT * g:GT * (g + 1)],
                                        zev[:], waits=[(SE, zx_evac_se[m])], inc=ZS)
                    if debug:
                        zev_free_st = b.dma("scalar",
                                            dram["dbg_z"][m, :, GT * g:GT * (g + 1)],
                                            zev[:], inc=ZS)
                else:
                    c = m - 16
                    hin = b.op("vector", lambda v, c=c: v.tensor_copy(
                        xg[:, 0:3], xghalo[:, 4 * c:4 * c + 3]),
                         waits=[(VE, halo_out_ve[c]), (SE, xg_free_se)], inc=VE)
                    def xv_(s, c=c, bank=bank):
                        return s.activation(xg[:, 3:3 + GT], pb[bank][:, :], AF.Identity,
                                            bias=w_inb_zx[:, 16 + c:17 + c])
                    xge = b.op("scalar", xv_, waits=[(PE, pe_zx), (VE, hin)], inc=SE)
                    zx_evac_se[m] = xge
                    halo_out_ve[c] = b.op("vector", lambda v, c=c: v.tensor_copy(
                        xghalo[:, 4 * c:4 * c + 3], xg[:, GT:GT + 3]),
                         waits=[(SE, xge)], inc=VE)
                    t0 = b.op("vector", lambda v, c=c: v.tensor_scalar(
                        cvy[:], xg[:, 0:GT], w_convw[:, 4 * c:4 * c + 1],
                        w_convb[:, c:c + 1], ALU.mult, ALU.add),
                         waits=[(SE, cvy_free_se), (SE, xge), (VE, hin)], inc=VE)
                    for j in range(1, 4):
                        def tap(v, c=c, j=j):
                            return v.scalar_tensor_tensor(
                                cvy[:], xg[:, j:j + GT],
                                w_convw[:, 4 * c + j:4 * c + j + 1],
                                cvy[:], ALU.mult, ALU.add)
                        t0 = b.op("vector", tap, waits=[(VE, t0)], inc=VE)
                    taps_ve = t0
                    se_xcv = b.op("scalar", lambda s: s.activation(
                        xcv[:], cvy[:], AF.Silu),
                         waits=[(VE, taps_ve), (XCS, xcv_free_st), (PE, xcv_free_pe)],
                         inc=SE)
                    cvy_free_se = se_xcv
                    xg_free_se = se_xcv
                    xcv_free_st = b.dma("scalar",
                                        dram["sp_xc"][c, :, GT * g:GT * (g + 1)],
                                        xcv[:], waits=[(SE, se_xcv)], inc=XCS)
                    if debug:
                        xcv_free_st = b.dma("scalar",
                                            dram["dbg_xc"][c, :, GT * g:GT * (g + 1)],
                                            xcv[:], inc=XCS)
                    for u in range(4):
                        def gmm(t, c=c, u=u):
                            return t.matmul(pb[u][:, 0:112],
                                            xcv[:, 128 * u:128 * (u + 1)],
                                            w_gatew[:, 112 * c:112 * (c + 1)],
                                            start=(c == 0), stop=False)
                        xcv_free_pe = b.op("tensor", gmm,
                                           waits=[(SE, se_xcv)] if u == 0 else [],
                                           inc=PE if u == 3 else None) or xcv_free_pe
            for u in range(4):
                ti = g * 4 + u
                def gb(t, u=u):
                    return t.matmul(pb[u][:, 0:112], cOnes[0:1, 0:128], w_gateb[:],
                                    start=False, stop=True)
                pe_g = b.op("tensor", gb, inc=PE)
                b.op("vector", lambda v, ti=ti, u=u: v.tensor_copy(
                    gres[:, 112 * ti:112 * (ti + 1)], pb[u][:, 0:112]),
                     waits=[(PE, pe_g)], inc=VE)
                if debug:
                    b.dma("scalar", dram["dbg_gates"][ti, :, :],
                          gres[:, 112 * ti:112 * (ti + 1)],
                          waits=[(VE, VE.n)], inc=DBG)
            xcv_free_pe = PE.n
            xnrT_free_pe = PE.n

        if debug:
            for ti in range(NT):
                b.dma("sync", dram["dbg_n2k"][ti, :, :], n2k[:, H * ti:H * (ti + 1)],
                      waits=[(VE, VE.n)], inc=DBG)

    env = locals()
    if KSTOP >= 6:
        emit_p2(nc, b, es, dram, debug, env)
    if KSTOP >= 7:
        emit_p4(nc, b, es, dram, debug, env)
    if KSTOP >= 8:
        emit_p3(nc, b, es, dram, debug, env)

    if KSTOP < 8:
        with contextlib.ExitStack() as pzs:
            zz = pzs.enter_context(nc.sbuf_tensor("zzero", [128, DM], F32))
            b.op("vector", lambda v: v.memset(zz[:], 0.0), inc=VE)
            for ti in range(NT):
                b.dma("sync", dram["out"][128 * ti:128 * (ti + 1), :], zz[:],
                      waits=[(VE, VE.n)], inc=OUT)
    for name, c in list(b.sem.items()):
        if c.n > 0 and name not in ("PE", "VE", "SE", "GE"):
            b.wait("sync", c, c.n)



def emit_p2(nc, b, es, dram, debug, env):
    b.barrier()
    """Recurrence phase. See module docstring for the chunk algebra."""
    KSUB2 = int(os.environ.get("KSUB2", "99"))
    PE, VE, SE, GE = (b.sem[k] for k in ("PE", "VE", "SE", "GE"))
    pb = env["pb"]; arena = env["arena"]; cI = env["cI"]; cIb = env["cIb"]
    cMask = env["cMask"]; cOnes = env["cOnes"]; cFreqs = env["cFreqs"]
    cEdts = env["cEdts"]
    gres = env["gres"]; vres = env["vres"]; n2k = env["n2k"]; retr = env["retr"]
    st_re = env["st_re"]
    p1_pe = PE.n
    p1_ve = VE.n
    p1_se = SE.n

    with contextlib.ExitStack() as p2s:
        def sb2(name, shape, dtype=F32):
            return p2s.enter_context(nc.sbuf_tensor(name, list(shape), dtype))

        QW = b.ctr("QW")
        S2 = b.ctr("S2")
        KCH = [b.ctr("KCH0"), b.ctr("KCH1")]
        XCH = [b.ctr("XCH0"), b.ctr("XCH1")]
        DBG = b.ctr("DBG")

        cSel = sb2("cSel", (16, 2048))
        b.dma("sync", cSel[:], dram["c_sel"], inc=S2)
        s2tot = S2.n
        for r in range(NKI):
            b.dma("gpsimd", arena[:, 2048 * r:2048 * (r + 1)],
                  dram["q_w"][128 * r:128 * (r + 1), :],
                  waits=[(PE, p1_pe)] if r == 0 else [], inc=QW)
        qw_tot = QW.n

        kch = [sb2("kch0", (128, 2048), BF16), sb2("kch1", (128, 2048), BF16)]
        xcch = [sb2("xcch0", (128, 2048), BF16), sb2("xcch1", (128, 2048), BF16)]
        qtm = sb2("qtm", (128, 2048), BF16)
        qsq = sb2("qsq", (128, 512))
        n2q = sb2("n2q", (128, 16))
        kchT = sb2("kchT", (128, 128), BF16)
        qchT = sb2("qchT", (128, 128), BF16)
        gsc = [sb2("gsc0", (128, 16 * 28)), sb2("gsc1", (128, 16 * 28))]
        itmp = [sb2("itmp0", (128, 32), I32), sb2("itmp1", (128, 32), I32)]
        ftmp = [sb2("ftmp0", (128, 32)), sb2("ftmp1", (128, 32))]
        hmT1 = [sb2("hmT10", (128, 96)), sb2("hmT11", (128, 96))]
        hmT2 = [sb2("hmT20", (128, 16)), sb2("hmT21", (128, 16))]
        hm = [sb2("hm0", (96, 128)), sb2("hm1", (96, 128))]
        hm2 = [sb2("hm20", (16, 128)), sb2("hm21", (16, 128))]
        lph = [sb2("lph0", (16, 128)), sb2("lph1", (16, 128))]
        hmB = [sb2("hmB0", (16, 128)), sb2("hmB1", (16, 128))]
        lph2 = [sb2("lph20", (16, 128)), sb2("lph21", (16, 128))]
        ltrA = [sb2("ltrA0", (16, 128)), sb2("ltrA1", (16, 128))]
        ltrB = [sb2("ltrB0", (16, 128)), sb2("ltrB1", (16, 128))]
        lphA = [sb2("lphA0", (80, 128)), sb2("lphA1", (80, 128))]
        lphB = [sb2("lphB0", (16, 128)), sb2("lphB1", (16, 128))]
        tmb = [sb2("tmb0", (128, 80)), sb2("tmb1", (128, 80))]
        tmb2 = [sb2("tmb20", (128, 16)), sb2("tmb21", (128, 16))]
        expo2 = [sb2("expo0", (128, 256), BF16), sb2("expo1", (128, 256), BF16)]
        BTs = sb2("BTs", (128, 2048), BF16)
        BKQs = sb2("BKQs", (128, 2048), BF16)
        powTs = sb2("powTs", (128, 6 * 2048), BF16)
        xv = [sb2("xv0", (128, 32), BF16), sb2("xv1", (128, 32), BF16)]
        rtlb = sb2("rtlb", (128, 32), BF16)
        w32 = sb2("w32", (128, 32))
        w32d = [sb2("w32d0", (128, 16)), sb2("w32d1", (128, 16))]
        vdup = [sb2("vdup0", (128, 32)), sb2("vdup1", (128, 32))]
        e32 = sb2("e32", (128, 32))
        a32 = sb2("a32", (128, 32))
        b32 = sb2("b32", (128, 32))
        c32 = sb2("c32", (128, 32))
        d32 = sb2("d32", (128, 32))
        vg = [sb2("vg0", (128, 32)), sb2("vg1", (128, 32))]
        st_b = sb2("st_b", (128, 32), BF16)
        cph = [sb2("cph0", (128, 16)), sb2("cph1", (128, 16))]
        sph = [sb2("sph0", (128, 16)), sb2("sph1", (128, 16))]
        cphC = [sb2("cphC0", (128, 16)), sb2("cphC1", (128, 16))]
        sphC = [sb2("sphC0", (128, 16)), sb2("sphC1", (128, 16))]
        elr = [sb2("elr0", (128, 16)), sb2("elr1", (128, 16))]
        elrq = [sb2("elrq0", (128, 16)), sb2("elrq1", (128, 16))]
        edcr = [sb2("edcr0", (128, 16)), sb2("edcr1", (128, 16))]
        elC = [sb2("elC0", (128, 16)), sb2("elC1", (128, 16))]
        phC = [sb2("phC0", (128, 16)), sb2("phC1", (128, 16))]
        ngc = [sb2("ngc0", (128, 16)), sb2("ngc1", (128, 16))]
        names = ["alpha", "spdt", "dt", "ah", "tanhom", "omega", "wh", "p1",
                 "det", "w2", "a2", "numc", "rdet", "cc", "ss", "m2r", "lnm",
                 "rr", "lrho", "theta", "beta", "sp5", "lnrk", "lnrq", "sigB",
                 "cgate", "mag2", "vp"]

        def g7(q, c):
            return gres[:, 112 * c + 16 * q: 112 * c + 16 * (q + 1)]

        def sact(out, in_, f, waits=(), scale=None, bias=None):
            kw = {}
            if scale is not None:
                kw["scale"] = scale
            if bias is not None:
                kw["bias"] = bias
            return b.op("scalar", lambda s: s.activation(out, in_, f, **kw),
                        waits=waits, inc=SE)

        def vop(fn, waits=()):
            return b.op("vector", fn, waits=waits, inc=VE)

        def gop(fn, waits=()):
            return b.op("gpsimd", fn, waits=waits, inc=GE)

        def pmm(bank_ap, lhsT, rhs, waits=(), start=True, stop=True):
            return b.op("tensor", lambda t: t.matmul(bank_ap, lhsT, rhs,
                                                     start=start, stop=stop),
                        waits=waits, inc=PE)

        def dup(dst32, src16, waits=()):
            d3 = dst32.rearrange("p (a two) -> p a two", two=2)
            s3 = src16.rearrange("p a -> p a ()")
            g1 = gop(lambda g_: g_.tensor_copy(d3[:, :, 0:1], s3), waits=waits)
            return gop(lambda g_: g_.tensor_copy(d3[:, :, 1:2], s3), waits=[(GE, g1)])

        def swap(dst32, src32, waits=()):
            d3 = dst32.rearrange("p (a two) -> p a two", two=2)
            s3 = src32.rearrange("p (a two) -> p a two", two=2)
            g1 = gop(lambda g_: g_.tensor_copy(d3[:, :, 0:1], s3[:, :, 1:2]), waits=waits)
            return gop(lambda g_: g_.tensor_copy(d3[:, :, 1:2], s3[:, :, 0:1]),
                       waits=[(GE, g1)])

        z1_ = b.op("vector", lambda v: v.memset(hmT1[0][:], 0.0),
                   waits=[(VE, p1_ve)], inc=VE)
        z1_ = b.op("vector", lambda v: v.memset(hmT1[1][:], 0.0), inc=VE)
        z2_ = b.op("vector", lambda v: v.memset(lphA[0][:], 0.0), inc=VE)
        z2_ = b.op("vector", lambda v: v.memset(lphA[1][:], 0.0), inc=VE)
        stb_ge = gop(lambda g_: g_.tensor_copy(st_b[:], st_re[:]), waits=[(VE, z2_)])

        kch_pe = [p1_pe, p1_pe]
        xch_pe = [p1_pe, p1_pe]
        qtm_readers_pe = p1_pe
        qn_prev = p1_ve
        gsc_free = p1_ve
        pb_w = [0] * 8
        pb_r = [[] for _ in range(8)]

        def bank_write(i, waits):
            w = list(waits) + pb_r[i]
            pb_r[i].clear()
            return w

        def bank_written(i):
            pb_w[i] = PE.n

        def bank_read(i):
            return [(PE, pb_w[i])]

        def bank_did_read(i, ctr, val):
            pb_r[i].append((ctr, val))

        prev_state_ve = p1_ve
        prev_retr_ve = p1_ve
        prev_chunk_done = dict(se=p1_se, ve=p1_ve, ge=0, pe=p1_pe)
        xv_free_ve = [0, 0]
        expo_free_ve = [0, 0]

        ab_state = {}
        cd_state = {}
        ab_prev_se = [p1_se]

        def emit_ab(c):
            buf = c % 2
            kld = b.dma("sync", kch[buf][:], dram["sp_k"][128 * c:128 * (c + 1), :],
                        waits=[(b.sem["KS"], b.sem["KS"].n), (PE, kch_pe[buf])],
                        inc=KCH[buf])
            xld = b.dma("sync",
                        xcch[buf].rearrange("p (a t) -> p a t", t=128),
                        dram["sp_xc"][:, :, 128 * c:128 * (c + 1)].rearrange(
                            "a p t -> p a t"),
                        waits=[(b.sem["XCS"], b.sem["XCS"].n), (PE, xch_pe[buf])],
                        inc=XCH[buf])

            sq_se = 0
            qn_ve = 0
            q_ev = {}
            for n in range(4):
                bank = n % 2
                w = [(XCH[buf], xld), (QW, qw_tot)] + bank_write(bank, [])
                for r in range(NKI):
                    def qmm(t, r=r, n=n, bank=bank, buf=buf):
                        return t.matmul(pb[bank][:, :],
                                        xcch[buf][:, 128 * r:128 * (r + 1)],
                                        arena[:, 2048 * r + 512 * n:
                                              2048 * r + 512 * (n + 1)],
                                        start=(r == 0), stop=(r == NKI - 1))
                    b.op("tensor", qmm, waits=w if r == 0 else [],
                         inc=PE if r == NKI - 1 else None)
                bank_written(bank)
                ev = b.op("vector", lambda v, n=n, bank=bank: v.tensor_copy(
                    qtm[:, 512 * n:512 * (n + 1)], pb[bank][:, :]),
                     waits=bank_read(bank) + [(PE, qtm_readers_pe)]
                     + ([(GE, prev_chunk_done["ge"])] if n == 0 else []), inc=VE)
                bank_did_read(bank, VE, ev)
                q_ev[n] = ev
                sq_se = b.op("scalar", lambda s, n=n: s.activation(
                    qsq[:], qtm[:, 512 * n:512 * (n + 1)], AF.Square),
                     waits=[(VE, ev), (VE, qn_ve),
                            (SE, sq_se if n > 0 else ab_prev_se[0])]
                     + ([(GE, prev_chunk_done["ge"])] if n == 0 else []),
                     inc=SE)
                qn_ve = b.op("vector", lambda v, n=n: v.reduce_sum(
                    n2q[:, 4 * n:4 * (n + 1)],
                    qsq.rearrange("p (h d) -> p h d", d=128), AX.X),
                     waits=[(SE, sq_se)], inc=VE)
            qn_prev = qn_ve

            ab_state[c] = (kld, xld, q_ev, qn_ve)

        def emit_cd(c):
            px = c % 2
            col = {nm: gsc[px][:, 16 * i:16 * (i + 1)]
                   for i, nm in enumerate(names)}
            _kld, _xld, _qev, qn_prev = ab_state[c]
            gw0 = [(VE, prev_chunk_done["ve"]), (SE, prev_chunk_done["se"]),
                   (GE, prev_chunk_done["ge"])]

            def softplus(dst, src, neg, waits=()):
                a_ = sact(col["p1"], src, AF.Abs, waits=waits)
                e_ = sact(col["p1"], col["p1"], AF.Exp, scale=-1.0, waits=[(SE, a_)])
                l_ = sact(col["w2"], col["p1"], AF.Ln, bias=1.0, waits=[(SE, e_)])
                r_ = sact(col["a2"], src, AF.Relu,
                          scale=(-1.0 if neg else None), waits=[(SE, l_)])
                return vop(lambda v: v.tensor_tensor(dst, col["w2"], col["a2"],
                                                     ALU.add), waits=[(SE, r_)])

            va = softplus(col["alpha"], g7(0, c), False, waits=gw0)
            vb = softplus(col["spdt"], g7(4, c), False, waits=[(VE, va)])
            vsp5 = softplus(col["sp5"], g7(5, c), True, waits=[(VE, vb)])
            sth = sact(col["tanhom"], g7(1, c), AF.Tanh, waits=[(VE, vsp5)])
            srr = sact(col["rr"], g7(6, c), AF.Sigmoid, waits=[(SE, sth)])
            sbt = sact(col["beta"], g7(5, c), AF.Sigmoid, waits=[(SE, srr)])
            sgb = sact(col["sigB"], g7(2, c), AF.Sigmoid, waits=[(SE, sbt)])
            sgc = sact(col["cgate"], g7(3, c), AF.Sigmoid, waits=[(SE, sgb)])
            vdt = vop(lambda v, c=c: v.tensor_tensor(col["dt"], col["spdt"],
                                                     cEdts[:], ALU.mult),
                      waits=[(VE, vb)])
            vdt = vop(lambda v: v.tensor_scalar(col["dt"], col["dt"], 1e-4, None,
                                                ALU.add), waits=[(VE, vdt)])
            vah = vop(lambda v: v.scalar_tensor_tensor(col["ah"], col["dt"], 0.5,
                                                       col["alpha"], ALU.mult,
                                                       ALU.mult),
                      waits=[(VE, vdt), (VE, va)])
            vom = vop(lambda v: v.scalar_tensor_tensor(col["omega"], col["tanhom"],
                                                       1.0, cFreqs[:], ALU.add,
                                                       ALU.mult), waits=[(SE, sth)])
            vwh = vop(lambda v: v.scalar_tensor_tensor(col["wh"], col["dt"], 0.5,
                                                       col["omega"], ALU.mult,
                                                       ALU.mult), waits=[(VE, vom)])
            vp1 = vop(lambda v: v.tensor_scalar(col["p1"], col["ah"], 1.0, None,
                                                ALU.add), waits=[(VE, vah)])
            vdet = vop(lambda v: v.tensor_tensor(col["det"], col["p1"], col["p1"],
                                                 ALU.mult), waits=[(VE, vp1)])
            vw2 = vop(lambda v: v.tensor_tensor(col["w2"], col["wh"], col["wh"],
                                                ALU.mult), waits=[(VE, vwh)])
            vdet = vop(lambda v: v.tensor_tensor(col["det"], col["det"], col["w2"],
                                                 ALU.add), waits=[(VE, vdet), (VE, vw2)])
            va2 = vop(lambda v: v.tensor_tensor(col["a2"], col["ah"], col["ah"],
                                                ALU.mult), waits=[(VE, vah)])
            vnum = vop(lambda v: v.scalar_tensor_tensor(col["numc"], col["a2"], -1.0,
                                                        col["w2"], ALU.mult,
                                                        ALU.subtract),
                       waits=[(VE, va2), (VE, vw2)])
            vnum = vop(lambda v: v.tensor_scalar(col["numc"], col["numc"], 1.0, None,
                                                 ALU.add), waits=[(VE, vnum)])
            vrd = vop(lambda v: v.reciprocal(col["rdet"], col["det"]),
                      waits=[(VE, vdet)])
            vcc = vop(lambda v: v.tensor_tensor(col["cc"], col["numc"], col["rdet"],
                                                ALU.mult), waits=[(VE, vnum), (VE, vrd)])
            vss = vop(lambda v: v.scalar_tensor_tensor(col["ss"], col["wh"], 2.0,
                                                       col["rdet"], ALU.mult,
                                                       ALU.mult),
                      waits=[(VE, vwh), (VE, vrd)])
            vm2 = vop(lambda v: v.tensor_tensor(col["m2r"], col["cc"], col["cc"],
                                                ALU.mult), waits=[(VE, vcc)])
            vs2 = vop(lambda v: v.tensor_tensor(col["p1"], col["ss"], col["ss"],
                                                ALU.mult), waits=[(VE, vss)])
            vm2 = vop(lambda v: v.tensor_tensor(col["m2r"], col["m2r"], col["p1"],
                                                ALU.add), waits=[(VE, vm2), (VE, vs2)])
            vm2 = vop(lambda v: v.tensor_scalar(col["m2r"], col["m2r"], 1e-16, 1.0,
                                                ALU.max, ALU.min), waits=[(VE, vm2)])
            vrc = vop(lambda v: v.reciprocal(col["p1"], col["cc"]), waits=[(VE, vm2)])
            vra = vop(lambda v: v.tensor_tensor(col["p1"], col["ss"], col["p1"],
                                                ALU.mult), waits=[(VE, vrc)])
            sat = sact(col["w2"], col["p1"], AF.Arctan, waits=[(VE, vra), (SE, sgc)])
            vlt = vop(lambda v: v.tensor_scalar(col["a2"], col["cc"], 0.0, None,
                                                ALU.is_lt), waits=[(SE, sat)])
            vth = vop(lambda v: v.scalar_tensor_tensor(col["theta"], col["a2"], PI,
                                                       col["w2"], ALU.mult, ALU.add),
                      waits=[(VE, vlt)])
            slm = sact(col["lnm"], col["m2r"], AF.Ln, waits=[(VE, vm2), (SE, sat)])
            vlr = vop(lambda v: v.scalar_tensor_tensor(col["lrho"], col["rr"], 4.0,
                                                       col["lnm"], ALU.mult,
                                                       ALU.mult),
                      waits=[(SE, slm)])
            sm2 = sact(col["mag2"], col["lrho"], AF.Exp, waits=[(VE, vlr)])
            vm4 = vop(lambda v: v.tensor_tensor(col["p1"], col["mag2"], col["mag2"],
                                                ALU.mult), waits=[(SE, sm2)])
            vc1 = vop(lambda v: v.tensor_scalar(col["p1"], col["p1"], -1.0, 1.0,
                                                ALU.mult, ALU.add), waits=[(VE, vm4)])
            vc1 = vop(lambda v: v.tensor_scalar(col["p1"], col["p1"], 0.0, None,
                                                ALU.max), waits=[(VE, vc1)])
            sq1 = sact(col["w2"], col["p1"], AF.Ln, bias=1e-6, waits=[(VE, vc1)])
            ssq_ = sact(col["w2"], col["w2"], AF.Exp, scale=0.5, waits=[(SE, sq1)])
            vvp = vop(lambda v: v.tensor_tensor(col["vp"], col["sigB"], col["w2"],
                                                ALU.mult), waits=[(SE, ssq_)])
            s1 = sact(col["p1"], n2k[:, 16 * c:16 * (c + 1)], AF.Ln, waits=[(VE, vvp)])
            s2_ = sact(col["p1"], col["p1"], AF.Exp, scale=0.5, waits=[(SE, s1)])
            s3 = sact(col["lnrk"], col["p1"], AF.Ln, bias=1e-6, waits=[(SE, s2_)])
            s4 = sact(col["w2"], n2q[:], AF.Ln, waits=[(SE, s3), (VE, qn_prev)])
            s5 = sact(col["w2"], col["w2"], AF.Exp, scale=0.5, waits=[(SE, s4)])
            s6 = sact(col["lnrq"], col["w2"], AF.Ln, bias=1e-6, waits=[(SE, s5)])
            ab_prev_se[0] = s6
            gd = dup(vdup[px], col["vp"],
                     waits=[(VE, vvp), (GE, prev_chunk_done["ge"]),
                            (VE, prev_chunk_done["ve"])])
            vvg = vop(lambda v, c=c, px=px: v.tensor_tensor(
                vg[px][:], vres[:, 32 * c:32 * (c + 1)], vdup[px][:], ALU.mult),
                      waits=[(GE, gd)])

            vh1 = vop(lambda v: v.tensor_copy(hmT1[px][:, 0:16], col["lrho"]),
                      waits=[(VE, vlr), (SE, prev_chunk_done["se"])])
            vh2 = vop(lambda v: v.tensor_tensor(hmT1[px][:, 32:48], col["sp5"],
                                                col["lnrk"], ALU.add),
                      waits=[(VE, vsp5), (SE, s3)])
            vh3 = vop(lambda v: v.tensor_copy(hmT1[px][:, 64:80], col["theta"]),
                      waits=[(VE, vth)])
            vh4 = vop(lambda v: v.tensor_copy(hmT2[px][:], col["lnrq"]),
                      waits=[(SE, s6)])
            pt1 = pmm(pb[2][0:96, 0:128], hmT1[px][:], cI[:],
                      waits=bank_write(2, [(VE, vh1), (VE, vh2), (VE, vh3)]))
            bank_written(2)
            vhm = vop(lambda v: v.tensor_copy(hm[px][:], pb[2][0:96, 0:128]),
                      waits=bank_read(2))
            bank_did_read(2, VE, vhm)
            pt2 = pmm(pb[3][0:16, 0:128], hmT2[px][:], cI[:],
                      waits=bank_write(3, [(VE, vh4)]))
            bank_written(3)
            vhm2 = vop(lambda v: v.tensor_copy(hm2[px][:], pb[3][0:16, 0:128]),
                       waits=bank_read(3))
            bank_did_read(3, VE, vhm2)
            vsc1 = vop(lambda v: v.tensor_tensor_scan(lph[px][:], hm[px][0:16, :],
                                                      hm[px][0:16, :], 0.0, ALU.add,
                                                      ALU.bypass), waits=[(VE, vhm)])
            vsc2 = vop(lambda v: v.tensor_tensor_scan(lph2[px][:], hm[px][64:80, :],
                                                      hm[px][64:80, :], 0.0, ALU.add,
                                                      ALU.bypass), waits=[(VE, vhm)])
            vcpB = vop(lambda v: v.tensor_copy(hmB[px][:], hm[px][32:48, :]),
                       waits=[(VE, vhm)])
            vraw = vop(lambda v: v.tensor_tensor(ltrA[px][:], lph[px][:], hmB[px][:],
                                                 ALU.subtract),
                       waits=[(VE, vsc1), (VE, vcpB)])
            vrbw = vop(lambda v: v.tensor_tensor(ltrB[px][:], lph[px][:], hm2[px][:],
                                                 ALU.subtract),
                       waits=[(VE, vsc1), (VE, vhm2)])
            vla1 = vop(lambda v: v.tensor_copy(lphA[px][0:16, :], lph[px][:]),
                       waits=[(VE, vsc1)])
            vla2 = vop(lambda v: v.tensor_copy(lphA[px][32:48, :], lph2[px][:]),
                       waits=[(VE, vsc2)])
            vla3 = vop(lambda v: v.tensor_scalar(lphA[px][64:80, :], lph[px][:],
                                                 lph[px][:, 127:128], -1.0,
                                                 ALU.subtract, ALU.mult),
                       waits=[(VE, vsc1)])
            vlb = vop(lambda v: v.tensor_scalar(lphB[px][:], lph2[px][:],
                                                lph2[px][:, 127:128], -1.0,
                                                ALU.subtract, ALU.mult),
                      waits=[(VE, vsc2)])
            pt3 = pmm(pb[2][:, 128:208], lphA[px][:], cI[0:80, 0:80],
                      waits=bank_write(2, [(VE, vla1), (VE, vla2), (VE, vla3)]))
            bank_written(2)
            vtb = vop(lambda v: v.tensor_copy(tmb[px][:], pb[2][:, 128:208]),
                      waits=bank_read(2))
            bank_did_read(2, VE, vtb)
            pt4 = pmm(pb[3][:, 128:144], lphB[px][:], cI[0:16, 0:16],
                      waits=bank_write(3, [(VE, vlb)]))
            bank_written(3)
            vtb2 = vop(lambda v: v.tensor_copy(tmb2[px][:], pb[3][:, 128:144]),
                       waits=bank_read(3))
            bank_did_read(3, VE, vtb2)
            vel = vop(lambda v: v.tensor_tensor(ftmp[px][:, 0:16], tmb[px][:, 0:16],
                                                col["lnrk"], ALU.subtract),
                      waits=[(VE, vtb)])
            se_elr = sact(elr[px][:], ftmp[px][:, 0:16], AF.Exp, waits=[(VE, vel)])
            vel2 = vop(lambda v: v.tensor_tensor(ftmp[px][:, 16:32], tmb[px][:, 0:16],
                                                 col["lnrq"], ALU.subtract),
                       waits=[(VE, vtb)])
            se_elrq = sact(elrq[px][:], ftmp[px][:, 16:32], AF.Exp, waits=[(VE, vel2)])
            vel3 = vop(lambda v: v.tensor_tensor(w32d[px][:, 0:16], tmb[px][:, 64:80],
                                                 col["lnrk"], ALU.subtract),
                       waits=[(VE, vtb), (SE, prev_chunk_done["se"])])
            se_edcr = sact(edcr[px][:], w32d[px][:, 0:16], AF.Exp, waits=[(VE, vel3)])
            velC = vop(lambda v: v.tensor_tensor(elC[px][:], tmb[px][:, 64:80],
                                                 tmb[px][:, 0:16], ALU.add),
                       waits=[(VE, vtb)])
            vngc = vop(lambda v: v.tensor_tensor(ngc[px][:], tmb[px][:, 0:16], col["lnrk"],
                                                 ALU.add), waits=[(VE, vtb)])
            vngc = vop(lambda v: v.tensor_scalar(ngc[px][:], ngc[px][:], -1.0, None,
                                                 ALU.mult), waits=[(VE, vngc)])
            def sincos(cdst, sdst, phi_src, waits):
                def branch(dst, off, waits):
                    if off is None:
                        w1 = vop(lambda v: v.tensor_scalar(
                            ftmp[px][:, 0:16], phi_src, 1.0 / (2 * PI), None,
                            ALU.mult), waits=waits)
                    else:
                        w1 = vop(lambda v: v.tensor_scalar(
                            ftmp[px][:, 0:16], phi_src, 1.0 / (2 * PI), off,
                            ALU.mult, ALU.add), waits=waits)
                    w2_ = vop(lambda v: v.tensor_copy(itmp[px][:, 0:16], ftmp[px][:, 0:16]),
                              waits=[(VE, w1)])
                    w3 = vop(lambda v: v.tensor_copy(ftmp[px][:, 16:32], itmp[px][:, 0:16]),
                             waits=[(VE, w2_)])
                    w4 = vop(lambda v: v.tensor_tensor(ftmp[px][:, 0:16], ftmp[px][:, 0:16],
                                                       ftmp[px][:, 16:32], ALU.subtract),
                             waits=[(VE, w3)])
                    g1 = vop(lambda v: v.tensor_scalar(w32d[px][:, 0:16], ftmp[px][:, 0:16],
                                                       0.5, None, ALU.is_ge),
                             waits=[(VE, w4)])
                    g2 = vop(lambda v: v.scalar_tensor_tensor(
                        ftmp[px][:, 0:16], w32d[px][:, 0:16], -1.0, ftmp[px][:, 0:16],
                        ALU.mult, ALU.add), waits=[(VE, g1)])
                    g3 = vop(lambda v: v.tensor_scalar(w32d[px][:, 0:16], ftmp[px][:, 0:16],
                                                       -0.5, None, ALU.is_lt),
                             waits=[(VE, g2)])
                    g4 = vop(lambda v: v.tensor_tensor(ftmp[px][:, 0:16], ftmp[px][:, 0:16],
                                                       w32d[px][:, 0:16], ALU.add),
                             waits=[(VE, g3)])
                    s_ = sact(dst, ftmp[px][:, 0:16], AF.Sin, scale=float(2 * PI),
                              waits=[(VE, g4)])
                    return vop(lambda v: v.tensor_copy(ftmp[px][:, 16:32], dst),
                               waits=[(SE, s_)])
                sv = branch(sdst, None, list(waits))
                cv = branch(cdst, 0.25, [(VE, sv)])
                return cv

            v_sc = sincos(cph[px][:], sph[px][:], tmb[px][:, 32:48], waits=[(VE, vtb), (SE, se_edcr)])
            vphC = vop(lambda v: v.tensor_tensor(phC[px][:], tmb2[px][:], tmb[px][:, 32:48],
                                                 ALU.add), waits=[(VE, vtb2), (VE, v_sc)])
            if debug and c < 2:
                DBGd = b.ctr("DBG")
                b.dma("scalar", dram["dbg2_sph0"][c], sph[px][:], waits=[(VE, v_sc)],
                      inc=DBGd)
                b.wait("vector", DBGd, DBGd.n)
            v_scC = sincos(cphC[px][:], sphC[px][:], phC[px][:], waits=[(VE, vphC)])

            cd_state[c] = dict(px=px, col=col, vraw=vraw, vrbw=vrbw,
                               vngc=vngc, v_sc=v_sc, v_scC=v_scC, sbt=sbt,
                               sgc=sgc, se_elr=se_elr, se_elrq=se_elrq,
                               se_edcr=se_edcr, velC=velC, vvg=vvg)

        def chunk_tail(c):
            nonlocal qtm_readers_pe, gsc_free, prev_retr_ve, prev_chunk_done
            nonlocal prev_state_ve, stb_ge

            buf = c % 2
            kld, xld, q_ev, _qn = ab_state[c]
            if KSUB2 < 1:
                kch_pe[buf] = PE.n
                xch_pe[buf] = PE.n
                qtm_readers_pe = PE.n
                return

            cd = cd_state[c]
            px = cd["px"]
            col = cd["col"]
            vraw = cd["vraw"]; vrbw = cd["vrbw"]; vngc = cd["vngc"]
            v_sc = cd["v_sc"]; v_scC = cd["v_scC"]; sbt = cd["sbt"]
            sgc = cd["sgc"]; se_elr = cd["se_elr"]; se_elrq = cd["se_elrq"]
            se_edcr = cd["se_edcr"]; velC = cd["velC"]; vvg = cd["vvg"]
            if KSUB2 < 2:
                kch_pe[buf] = PE.n
                xch_pe[buf] = PE.n
                qtm_readers_pe = PE.n
                gsc_free = VE.n
                prev_chunk_done = dict(se=SE.n, ve=VE.n, ge=GE.n, pe=PE.n)
                return

            kchTs = arena[:, 36864:38912]
            qchTs = arena[:, 38912:40960]
            nAs = [arena[:, 32768:34816], arena[:, 34816:36864]]
            m1_ve = {}
            kt_ve = {}
            qt_ve = {}
            n0_ve = {}
            for h in range(16):
                p = h % 2
                p_kt = pmm(pb[p][:, 0:128], kch[buf][:, 128 * h:128 * (h + 1)],
                           cIb[:], waits=bank_write(p, [(KCH[buf], kld)]))
                p_qt = pmm(pb[p][:, 128:256], qtm[:, 128 * h:128 * (h + 1)],
                           cIb[:], waits=[(VE, q_ev[h // 4])])
                bank_written(p)
                kt_ve[h] = b.op("vector", lambda v, h=h, p=p: v.tensor_copy(
                    kchTs[:, 128 * h:128 * (h + 1)], pb[p][:, 0:128]),
                    waits=bank_read(p), inc=VE)
                qt_ve[h] = b.op("vector", lambda v, h=h, p=p: v.tensor_copy(
                    qchTs[:, 128 * h:128 * (h + 1)], pb[p][:, 128:256]),
                    waits=[], inc=VE)
                bank_did_read(p, VE, qt_ve[h])
                p_g1 = pmm(pb[2 + p][:, 0:128], kchTs[:, 128 * h:128 * (h + 1)],
                           kchTs[:, 128 * h:128 * (h + 1)],
                           waits=bank_write(2 + p, [(VE, kt_ve[h])]))
                p_g2 = pmm(pb[2 + p][:, 128:256], kchTs[:, 128 * h:128 * (h + 1)],
                           qchTs[:, 128 * h:128 * (h + 1)],
                           waits=[(VE, qt_ve[h])])
                bank_written(2 + p)
                p_e1 = pmm(pb[4 + p][:, 0:128], cSel[:, 128 * h:128 * (h + 1)],
                           ltrA[px][:],
                           waits=bank_write(4 + p, [(VE, vraw), (VE, vrbw),
                                                    (S2, s2tot)]),
                           stop=False)
                p_m1 = pmm(pb[4 + p][:, 0:128], cIb[:], cMask[:, 0:128],
                           start=False, stop=True)
                p_e2 = pmm(pb[4 + p][:, 128:256], cSel[:, 128 * h:128 * (h + 1)],
                           ltrB[px][:], stop=False)
                p_m2 = pmm(pb[4 + p][:, 128:256], cIb[:], cMask[:, 128:256],
                           start=False, stop=True)
                bank_written(4 + p)
                exq = expo2[p]
                s_ex = b.op("scalar", lambda s, h=h, p=p, exq=exq: s.activation(
                    exq[:], pb[4 + p][:, 0:256], AF.Exp, bias=ngc[px][:, h:h + 1]),
                    waits=bank_read(4 + p) + [(VE, vngc), (VE, expo_free_ve[p]),
                                              (GE, stb_ge)],
                    inc=SE)
                bank_did_read(4 + p, SE, s_ex)
                v_m1 = b.op("vector", lambda v, h=h, p=p, exq=exq: v.tensor_tensor(
                    BTs[:, 128 * h:128 * (h + 1)], exq[:, 0:128],
                    pb[2 + p][:, 0:128], ALU.mult),
                    waits=bank_read(2 + p) + [(SE, s_ex)], inc=VE)
                v_m2 = b.op("vector", lambda v, h=h, p=p, exq=exq: v.tensor_tensor(
                    BKQs[:, 128 * h:128 * (h + 1)], exq[:, 128:256],
                    pb[2 + p][:, 128:256], ALU.mult),
                    waits=[(SE, s_ex)], inc=VE)
                bank_did_read(2 + p, VE, v_m2)
                expo_free_ve[p] = v_m2
                m1_ve[h] = v_m1
                p_n0 = pmm(pb[6 + p][:, 0:128], BTs[:, 128 * h:128 * (h + 1)],
                           cIb[:], waits=bank_write(6 + p, [(VE, v_m1)]))
                bank_written(6 + p)
                n0_ve[h] = b.op("vector", lambda v, h=h, p=p: v.tensor_copy(
                    nAs[0][:, 128 * h:128 * (h + 1)], pb[6 + p][:, 0:128]),
                    waits=bank_read(6 + p), inc=VE)
                bank_did_read(6 + p, VE, n0_ve[h])
                p_w = pmm(pb[3][:, 288 + 2 * h:290 + 2 * h],
                          kchTs[:, 128 * h:128 * (h + 1)], st_b[:, 2 * h:2 * h + 2],
                          waits=(bank_write(3, [(GE, stb_ge), (VE, kt_ve[h])])
                                 if h == 0 else [(GE, stb_ge)]))
                p_wq = pmm(pb[3][:, 320 + 2 * h:322 + 2 * h],
                           qchTs[:, 128 * h:128 * (h + 1)], st_b[:, 2 * h:2 * h + 2],
                           waits=[(VE, qt_ve[h])])
                if h == 15:
                    bank_written(3)
            qtm_readers_pe = PE.n
            nA_ve = {h: (VE, v) for h, v in n0_ve.items()}
            tt_ve = {h: (VE, v) for h, v in m1_ve.items()}
            for k in range(6):
                nxt_ve = {}
                t_ve = {}
                for h in range(16):
                    p = h % 2
                    Tsrc = (BTs[:, 128 * h:128 * (h + 1)] if k == 0 else
                            powTs[:, ((k - 1) * 16 + h) * 128:((k - 1) * 16 + h) * 128 + 128])
                    Nsrc = nAs[k % 2][:, 128 * h:128 * (h + 1)]
                    p_t = pmm(pb[6 + p][:, 256:384], Nsrc, Tsrc,
                              waits=bank_write(6 + p, [nA_ve[h], tt_ve[h]]))
                    bank_written(6 + p)
                    if k < 5:
                        p_n = pmm(pb[4 + p][:, 256:384], Tsrc, Nsrc,
                                  waits=bank_write(4 + p, []))
                        bank_written(4 + p)
                    dstT = powTs[:, (k * 16 + h) * 128:(k * 16 + h) * 128 + 128]
                    e_t = b.op("vector", lambda v, dstT=dstT, p=p: v.tensor_copy(
                        dstT, pb[6 + p][:, 256:384]),
                        waits=bank_read(6 + p), inc=VE)
                    bank_did_read(6 + p, VE, e_t)
                    t_ve[h] = (VE, e_t)
                    if k < 5:
                        dstN = nAs[(k + 1) % 2][:, 128 * h:128 * (h + 1)]
                        e_n = b.op("scalar", lambda s, dstN=dstN, p=p: s.activation(
                            dstN, pb[4 + p][:, 256:384], AF.Identity),
                            waits=bank_read(4 + p), inc=SE)
                        bank_did_read(4 + p, SE, e_n)
                        nxt_ve[h] = (SE, e_n)
                nA_ve = nxt_ve
                tt_ve = t_ve

            if c + 1 < NT:
                emit_ab(c + 1)
                emit_cd(c + 1)

            v_wev = vop(lambda v: v.tensor_copy(w32[:], pb[3][:, 288:320]),
                        waits=bank_read(3))
            bank_did_read(3, VE, v_wev)
            g_el = dup(e32, elr[px][:], waits=[(SE, se_elr), (GE, GE.n)])
            v_wsc = vop(lambda v: v.tensor_tensor(w32[:], w32[:], e32[:], ALU.mult),
                        waits=[(VE, v_wev), (GE, g_el)])
            g_cd = dup(a32, cph[px][:], waits=[(VE, v_sc)])
            g_sw = swap(b32, vg[px][:], waits=[(VE, vvg)])
            g_s1 = dup(c32, sph[px][:], waits=[(VE, v_sc)])
            v_sn = vop(lambda v: v.tensor_scalar(
                c32.rearrange("p (a two) -> p a two", two=2)[:, :, 0:1],
                c32.rearrange("p (a two) -> p a two", two=2)[:, :, 0:1],
                -1.0, None, ALU.mult), waits=[(GE, g_s1)])
            if debug and c < 2:
                DBGe = b.ctr("DBG")
                b.dma("scalar", dram["dbg2_c32a"][c], c32[:], waits=[(GE, g_s1)],
                      inc=DBGe)
                b.wait("vector", DBGe, DBGe.n)
                b.wait("gpsimd", DBGe, DBGe.n)
            t1 = vop(lambda v: v.tensor_tensor(d32[:], a32[:], vg[px][:], ALU.mult),
                     waits=[(GE, g_cd), (VE, vvg)])
            t2 = vop(lambda v: v.tensor_tensor(b32[:], c32[:], b32[:], ALU.mult),
                     waits=[(VE, v_sn), (GE, g_sw)])
            if debug and c < 2:
                DBGc = b.ctr("DBG")
                b.dma("scalar", dram["dbg2_a32"][c], a32[:], waits=[(VE, t1)], inc=DBGc)
                b.dma("scalar", dram["dbg2_b32"][c], b32[:], waits=[(VE, t2)], inc=DBGc)
                b.dma("scalar", dram["dbg2_c32"][c], c32[:], waits=[(VE, t2)], inc=DBGc)
                b.dma("scalar", dram["dbg2_t1"][c], d32[:], waits=[(VE, t1)], inc=DBGc)
                b.wait("vector", DBGc, DBGc.n)
            t3 = vop(lambda v: v.tensor_tensor(d32[:], d32[:], b32[:], ALU.add),
                     waits=[(VE, t1), (VE, t2)])
            g_bd = dup(e32, col["beta"], waits=[(SE, sbt), (GE, g_el), (VE, v_wsc)])
            if debug and c < 2:
                b.dma("scalar", dram["dbg2_eiv"][c], d32[:],
                      waits=[(VE, t3)], inc=b.ctr("DBG"))
            t4 = vop(lambda v: v.tensor_tensor(d32[:], d32[:], w32[:], ALU.subtract),
                     waits=[(VE, t3), (VE, v_wsc), (b.ctr("DBG"), b.ctr("DBG").n)])
            t5 = vop(lambda v: v.tensor_tensor(d32[:], d32[:], e32[:], ALU.mult),
                     waits=[(VE, t4), (GE, g_bd)])
            v_rtb = vop(lambda v: v.tensor_copy(rtlb[:], d32[:]), waits=[(VE, t5)])
            if debug and c < 2:
                b.dma("scalar", dram["dbg2_rtl2"][c], rtlb[:],
                      waits=[(VE, v_rtb)], inc=DBG)

            xcur = rtlb
            xcur_ve = v_rtb
            for step in range(7):
                k = 5 - step if step < 6 else None
                for h in range(16):
                    if step < 6:
                        lhs = powTs[:, ((5 - step) * 16 + h) * 128:
                                    ((5 - step) * 16 + h) * 128 + 128]
                    else:
                        lhs = BTs[:, 128 * h:128 * (h + 1)]
                    p_mv = pmm(pb[3][:, 2 * h:2 * h + 2], lhs,
                               xcur[:, 2 * h:2 * h + 2],
                               waits=(bank_write(3, [(VE, xcur_ve)])
                                      if h == 0 else []))
                    del p_mv
                bank_written(3)
                nxt = xv[step % 2]
                op_ = ALU.add if step < 6 else ALU.subtract
                x_ve = vop(lambda v, nxt=nxt, xcur=xcur, op_=op_: v.tensor_tensor(
                    nxt[:], xcur[:], pb[3][:, 0:32], op_),
                           waits=bank_read(3) + [(VE, xv_free_ve[step % 2]),
                                                 (PE, prev_chunk_done["pe"])])
                bank_did_read(3, VE, x_ve)
                xv_free_ve[step % 2] = x_ve
                xcur = nxt
                xcur_ve = x_ve
            uhat = xcur
            uhat_ve = xcur_ve

            for h in range(16):
                p_y = pmm(pb[3][:, 96 + 2 * h:98 + 2 * h],
                          BKQs[:, 128 * h:128 * (h + 1)],
                          uhat[:, 2 * h:2 * h + 2],
                          waits=(bank_write(3, [(VE, uhat_ve), (VE, v_m2)])
                                 if h == 0 else []))
            bank_written(3)
            v_ye = vop(lambda v: v.tensor_copy(a32[:], pb[3][:, 96:128]),
                       waits=bank_read(3))
            v_wqe = vop(lambda v: v.tensor_copy(b32[:], pb[3][:, 320:352]),
                        waits=bank_read(3))
            bank_did_read(3, VE, v_ye)
            bank_did_read(3, VE, v_wqe)
            g_eq = dup(c32, elrq[px][:], waits=[(SE, se_elrq), (VE, t2)])
            y1 = vop(lambda v: v.scalar_tensor_tensor(b32[:], b32[:], 1.0, c32[:],
                                                      ALU.mult, ALU.mult),
                     waits=[(VE, v_wqe), (GE, g_eq)])
            y2 = vop(lambda v: v.tensor_tensor(a32[:], a32[:], b32[:], ALU.add),
                     waits=[(VE, v_ye), (VE, y1)])
            g_ys = swap(d32, a32[:], waits=[(VE, y2)])
            g_cd2 = dup(e32, cph[px][:], waits=[(GE, g_eq), (VE, t5)])
            y3 = vop(lambda v: v.tensor_tensor(e32[:], e32[:], a32[:], ALU.mult),
                     waits=[(GE, g_cd2), (VE, y2)])
            g_sp2 = dup(c32, sph[px][:], waits=[(VE, y3)])
            v_sn2 = vop(lambda v: v.tensor_scalar(
                c32.rearrange("p (a two) -> p a two", two=2)[:, :, 1:2],
                c32.rearrange("p (a two) -> p a two", two=2)[:, :, 1:2],
                -1.0, None, ALU.mult), waits=[(GE, g_sp2)])
            y4 = vop(lambda v: v.tensor_tensor(d32[:], c32[:], d32[:], ALU.mult),
                     waits=[(VE, v_sn2), (GE, g_ys)])
            y5 = vop(lambda v: v.tensor_tensor(a32[:], e32[:], d32[:], ALU.add),
                     waits=[(VE, y3), (VE, y4)])
            g_cg = dup(b32, col["cgate"], waits=[(SE, sgc), (GE, g_cd2), (VE, y2)])
            y6 = vop(lambda v, c=c: v.tensor_tensor(
                retr[:, 32 * c:32 * (c + 1)], a32[:], b32[:], ALU.mult),
                     waits=[(VE, y5), (GE, g_cg)])
            if debug:
                retr_dbg = b.dma("scalar", dram["dbg_retr"][128 * c:128 * (c + 1), :],
                                 retr[:, 32 * c:32 * (c + 1)],
                                 waits=[(VE, y6)], inc=DBG)

            g_ed = dup(c32, edcr[px][:], waits=[(SE, se_edcr), (GE, g_cg), (VE, y4)])
            u1 = vop(lambda v, uhat=uhat: v.tensor_tensor(d32[:], uhat[:], c32[:],
                                                          ALU.mult),
                     waits=[(VE, uhat_ve), (GE, g_ed), (VE, y6)])
            v_upr = vop(lambda v: v.tensor_copy(rtlb[:], d32[:]), waits=[(VE, u1)])
            for h in range(16):
                p_st = pmm(pb[3][:, 128 + 2 * h:130 + 2 * h],
                           kch[buf][:, 128 * h:128 * (h + 1)],
                           rtlb[:, 2 * h:2 * h + 2],
                           waits=(bank_write(3, [(VE, v_upr)]) if h == 0 else []))
            bank_written(3)
            v_sd = vop(lambda v: v.tensor_copy(d32[:], pb[3][:, 128:160]),
                       waits=bank_read(3))
            bank_did_read(3, VE, v_sd)
            s_elC = sact(w32[:, 0:16], elC[px][:], AF.Exp,
                         waits=[(VE, velC), (VE, v_rtb)])
            g_elC = dup(e32, w32[:, 0:16], waits=[(SE, s_elC), (GE, g_ed), (VE, y5)])
            g_cC = dup(a32, cphC[px][:], waits=[(VE, v_scC), (GE, g_elC), (VE, y6)])
            g_sC = dup(b32, sphC[px][:], waits=[(VE, v_scC), (GE, g_cC), (VE, y4)])
            v_snC = vop(lambda v: v.tensor_scalar(
                b32.rearrange("p (a two) -> p a two", two=2)[:, :, 1:2],
                b32.rearrange("p (a two) -> p a two", two=2)[:, :, 1:2],
                -1.0, None, ALU.mult), waits=[(GE, g_sC)])
            g_ss = swap(c32, st_re[:], waits=[(VE, prev_state_ve), (GE, g_sC), (VE, u1)])
            z1 = vop(lambda v: v.tensor_tensor(c32[:], b32[:], c32[:], ALU.mult),
                     waits=[(VE, v_snC), (GE, g_ss)])
            z2 = vop(lambda v: v.tensor_tensor(w32[:], a32[:], st_re[:], ALU.mult),
                     waits=[(GE, g_cC), (VE, prev_state_ve)])
            z3 = vop(lambda v: v.tensor_tensor(w32[:], w32[:], c32[:], ALU.add),
                     waits=[(VE, z1), (VE, z2)])
            z4 = vop(lambda v: v.tensor_tensor(w32[:], w32[:], e32[:], ALU.mult),
                     waits=[(VE, z3), (GE, g_elC)])
            g_sd = swap(c32, d32[:], waits=[(VE, z3), (VE, v_sd)])
            z5 = vop(lambda v: v.tensor_tensor(c32[:], b32[:], c32[:], ALU.mult),
                     waits=[(GE, g_sd)])
            z6 = vop(lambda v: v.tensor_tensor(d32[:], a32[:], d32[:], ALU.mult),
                     waits=[(VE, v_sd), (VE, z4), (GE, g_sd)])
            z7 = vop(lambda v: v.tensor_tensor(d32[:], d32[:], c32[:], ALU.add),
                     waits=[(VE, z5), (VE, z6)])
            z8 = vop(lambda v: v.tensor_tensor(st_re[:], w32[:], d32[:], ALU.add),
                     waits=[(VE, z7)])
            prev_state_ve = z8
            stb_ge = gop(lambda g_: g_.tensor_copy(st_b[:], st_re[:]),
                         waits=[(VE, z8)])

            if debug and c < 2:
                for nm, tile, wv in [
                        ("dbg2_gsc", gsc[px][:], (VE, VE.n)),
                        ("dbg2_tmb", tmb[px][:], (VE, VE.n)),
                        ("dbg2_bts", BTs[:], (VE, VE.n)),
                        ("dbg2_bkqs", BKQs[:], (VE, VE.n)),
                        ("dbg2_uhat", uhat[:], (VE, VE.n)),
                        ("dbg2_rtlb", rtlb[:], (VE, VE.n)),
                        ("dbg2_n2q", n2q[:], (VE, VE.n)),
                        ("dbg2_vg", vg[px][:], (VE, VE.n)),
                        ("dbg2_vres", vres[:, 32 * c:32 * (c + 1)], (VE, VE.n))]:
                    b.dma("scalar", dram[nm][c], tile, waits=[wv], inc=DBG)
                cl = b.op("vector", lambda v: v.tensor_copy(ftmp[px][:, 0:16],
                                                             cph[px][:]), inc=VE)
                cl = b.op("vector", lambda v: v.tensor_copy(ftmp[px][:, 16:32],
                                                            sph[px][:]),
                          waits=[(VE, cl)], inc=VE)
                b.dma("scalar", dram["dbg2_cols"][c, :, 0:32], ftmp[px][:],
                      waits=[(VE, cl)], inc=DBG)
                for i, t2 in enumerate([elr, elrq, edcr, elC, ngc, cphC, sphC]):
                    b.dma("scalar", dram["dbg2_cols"][c, :, 32 + 16 * i:48 + 16 * i],
                          t2[px][:], waits=[(VE, VE.n), (SE, SE.n)], inc=DBG)
            kch_pe[buf] = PE.n
            xch_pe[buf] = PE.n
            gsc_free = VE.n
            prev_retr_ve = y6
            prev_chunk_done = dict(se=SE.n, ve=VE.n, ge=GE.n, pe=PE.n)


        emit_ab(0)
        emit_cd(0)
        for c in range(NT):
            chunk_tail(c)

        env["p2_end"] = dict(pe=PE.n, ve=VE.n, se=SE.n)




def emit_p4(nc, b, es, dram, debug, env):
    b.barrier()
    """Attn post: readout -> groupnorm(full seq) -> silu(z)-gate -> +D*xc ->
    out_w -> + xn residual -> sp_s."""
    PE, VE, SE, GE = (b.sem[k] for k in ("PE", "VE", "SE", "GE"))
    pb = env["pb"]; arena = env["arena"]; cI = env["cI"]; cOnes = env["cOnes"]
    retr = env["retr"]; w_gnw = env["w_gnw"]; w_gnb = env["w_gnb"]
    w_dpar = env["w_dpar"]
    p2_pe = PE.n
    p2_ve = VE.n
    p2_se = SE.n
    DBG = b.ctr("DBG")

    with contextlib.ExitStack() as p4s:
        def sb4(name, shape, dtype=F32):
            return p4s.enter_context(nc.sbuf_tensor(name, list(shape), dtype))

        S4 = b.ctr("S4")
        OW = b.ctr("OW")
        ZL = [b.ctr("ZL0"), b.ctr("ZL1")]
        XT = [b.ctr("XT0"), b.ctr("XT1")]
        XN4 = [b.ctr("XN40"), b.ctr("XN41")]
        SST = b.ctr("SST")

        w_rout = sb4("w_rout", (32, 2048), BF16)
        b.dma("gpsimd", w_rout[:], dram["readout_w"], inc=S4)
        s4tot = S4.n
        for r in range(NKI):
            b.dma("gpsimd", arena[:, 32768 + 1024 * r:32768 + 1024 * (r + 1)],
                  dram["out_w"][128 * r:128 * (r + 1), :],
                  waits=[(PE, p2_pe)] if r == 0 else [], inc=OW)
        ow_tot = OW.n

        retrT = sb4("retrT", (32, 2048), BF16)
        ysq = sb4("ysq", (128, 513))
        gn1 = sb4("gn1", (128, 16))
        gn2 = sb4("gn2", (128, 16))
        grow = sb4("grow", (16, 4))
        grow2 = sb4("grow2", (1, 32))
        gnm = sb4("gnm", (128, 16))
        gnr = sb4("gnr", (128, 16))
        scol = sb4("scol", (128, 16))
        bcol = sb4("bcol", (128, 16))
        zt = [sb4("zt0", (128, 512), BF16), sb4("zt1", (128, 512), BF16)]
        zs = sb4("zs", (128, 512), BF16)
        xct = [sb4("xct0", (128, 512), BF16), sb4("xct1", (128, 512), BF16)]
        yfa = sb4("yfa", (128, 16 * 512), BF16)
        sxn = [sb4("sxn0", (128, DM)), sb4("sxn1", (128, DM))]
        sout = sb4("sout", (128, DM))

        pb_w = [p2_pe] * 8
        pb_r = [[] for _ in range(8)]

        def bank_write(i, waits):
            w = list(waits) + pb_r[i]
            pb_r[i].clear()
            return w

        def bank_written(i):
            pb_w[i] = PE.n

        def bank_read(i):
            return [(PE, pb_w[i])]

        def bank_did_read(i, ctr, val):
            pb_r[i].append((ctr, val))

        def vop(fn, waits=()):
            return b.op("vector", fn, waits=waits, inc=VE)

        def sop(fn, waits=()):
            return b.op("scalar", fn, waits=waits, inc=SE)

        def pmm(bank_ap, lhsT, rhs, waits=(), start=True, stop=True):
            return b.op("tensor", lambda t: t.matmul(bank_ap, lhsT, rhs,
                                                     start=start, stop=stop),
                        waits=waits, inc=PE)

        tr_ve = {}
        for c in range(NT):
            bank = c % 2
            p_t = pmm(pb[bank][0:32, 0:128],
                      retr[:, 32 * c:32 * (c + 1)], cI[:],
                      waits=bank_write(bank, [(VE, p2_ve)] if c < 2 else []))
            bank_written(bank)
            tr_ve[c] = vop(lambda v, c=c, bank=bank: v.tensor_copy(
                retrT[:, 128 * c:128 * (c + 1)], pb[bank][0:32, 0:128]),
                waits=bank_read(bank))
            bank_did_read(bank, VE, tr_ve[c])
        retrT_ve = tr_ve[NT - 1]

        g1_ve = 0
        g2_ve = 0
        sq_se = 0
        for m in range(NKI):
            for n in range(4):
                bank = 2 + (m * 4 + n) % 2
                p_r = pmm(pb[bank][:, :], w_rout[:, 128 * m:128 * (m + 1)],
                          retrT[:, 512 * n:512 * (n + 1)],
                          waits=bank_write(bank, [(S4, s4tot), (VE, retrT_ve)]))
                bank_written(bank)
                ev = vop(lambda v, m=m, n=n, bank=bank: v.tensor_copy(
                    arena[:, 2048 * m + 512 * n:2048 * m + 512 * (n + 1)],
                    pb[bank][:, :]), waits=bank_read(bank))
                bank_did_read(bank, VE, ev)
                r1 = vop(lambda v, m=m, n=n: v.reduce_sum(
                    ysq[:, 512:513],
                    arena[:, 2048 * m + 512 * n:2048 * m + 512 * (n + 1)],
                    AX.X), waits=[(VE, ev)])
                if n == 0:
                    g1_ve = vop(lambda v, m=m: v.tensor_copy(gn1[:, m:m + 1],
                                                             ysq[:, 512:513]),
                                waits=[(VE, r1)])
                else:
                    g1_ve = vop(lambda v, m=m: v.tensor_tensor(
                        gn1[:, m:m + 1], gn1[:, m:m + 1], ysq[:, 512:513], ALU.add),
                        waits=[(VE, r1), (VE, g1_ve)])
                sq = sop(lambda s, m=m, n=n: s.activation(
                    ysq[:, 0:512], arena[:, 2048 * m + 512 * n:2048 * m + 512 * (n + 1)],
                    AF.Square), waits=[(VE, ev), (VE, g2_ve)])
                sq_se = sq
                r2 = vop(lambda v: v.reduce_sum(ysq[:, 512:513], ysq[:, 0:512], AX.X),
                         waits=[(SE, sq), (VE, g1_ve)])
                if n == 0:
                    g2_ve = vop(lambda v, m=m: v.tensor_copy(gn2[:, m:m + 1],
                                                             ysq[:, 512:513]),
                                waits=[(VE, r2)])
                else:
                    g2_ve = vop(lambda v, m=m: v.tensor_tensor(
                        gn2[:, m:m + 1], gn2[:, m:m + 1], ysq[:, 512:513], ALU.add),
                        waits=[(VE, r2), (VE, g2_ve)])

        p_s1 = pmm(pb[6][0:16, 0:1], gn1[:], cOnes[:, 0:1],
                   waits=bank_write(6, [(VE, g1_ve)]))
        p_s2 = pmm(pb[6][0:16, 1:2], gn2[:], cOnes[:, 0:1],
                   waits=[(VE, g2_ve)])
        bank_written(6)
        ge1 = vop(lambda v: v.tensor_copy(grow[:, 0:2], pb[6][0:16, 0:2]),
                  waits=bank_read(6))
        bank_did_read(6, VE, ge1)
        Nn = float(L * 128)
        ge2 = vop(lambda v: v.tensor_scalar(grow[:, 0:2], grow[:, 0:2], 1.0 / Nn,
                                            None, ALU.mult), waits=[(VE, ge1)])
        ge3 = vop(lambda v: v.scalar_tensor_tensor(grow[:, 2:3], grow[:, 0:1], 1.0,
                                                   grow[:, 0:1], ALU.mult, ALU.mult),
                  waits=[(VE, ge2)])
        ge4 = vop(lambda v: v.tensor_tensor(grow[:, 2:3], grow[:, 1:2], grow[:, 2:3],
                                            ALU.subtract), waits=[(VE, ge3)])
        ge5 = sop(lambda s: s.activation(grow[:, 3:4], grow[:, 2:3], AF.Ln,
                                         bias=1e-5), waits=[(VE, ge4), (SE, p2_se)])
        ge6 = sop(lambda s: s.activation(grow[:, 3:4], grow[:, 3:4], AF.Exp,
                                         scale=-0.5), waits=[(SE, ge5)])
        p_r1 = pmm(pb[6][0:1, 0:16], grow[:, 0:1], cI[0:16, 0:16],
                   waits=bank_write(6, [(SE, ge6)]))
        p_r2 = pmm(pb[6][0:1, 16:32], grow[:, 3:4], cI[0:16, 0:16])
        bank_written(6)
        ge7 = vop(lambda v: v.tensor_copy(grow2[:, 0:32], pb[6][0:1, 0:32]),
                  waits=bank_read(6))
        bank_did_read(6, VE, ge7)
        p_b1 = pmm(pb[6][:, 32:48], cOnes[0:1, 0:128], grow2[:, 0:16],
                   waits=bank_write(6, [(VE, ge7)]))
        p_b2 = pmm(pb[6][:, 48:64], cOnes[0:1, 0:128], grow2[:, 16:32])
        bank_written(6)
        ge8 = vop(lambda v: v.tensor_copy(gnm[:], pb[6][:, 32:48]),
                  waits=bank_read(6))
        ge9 = vop(lambda v: v.tensor_copy(gnr[:], pb[6][:, 48:64]),
                  waits=[(VE, ge8)])
        bank_did_read(6, VE, ge9)
        gs1 = vop(lambda v: v.tensor_tensor(scol[:], gnr[:], w_gnw[:], ALU.mult),
                  waits=[(VE, ge9)])
        gs2 = vop(lambda v: v.scalar_tensor_tensor(bcol[:], gnm[:], -1.0, scol[:],
                                                   ALU.mult, ALU.mult),
                  waits=[(VE, gs1)])
        gs3 = vop(lambda v: v.tensor_tensor(bcol[:], bcol[:], w_gnb[:], ALU.add),
                  waits=[(VE, gs2)])

        zld_prev = [0, 0]
        zs_free = [0]
        xcld_prev = [0, 0]
        zt_free = [0, 0]
        xct_free = [0, 0]
        yfa_free_pe = p2_pe
        sxn_free = [0, 0]
        sst_prev = 0
        sout_st = 0
        for n in range(4):
            for m in range(NKI):
                bufz = (n * NKI + m) % 2
                zld = b.dma("sync", zt[bufz][:],
                            dram["sp_z"][m, :, 512 * n:512 * (n + 1)],
                            waits=[(b.sem["ZS"], b.sem["ZS"].n), (SE, zt_free[bufz])],
                            inc=ZL[bufz])
                xcl = b.dma("sync", xct[bufz][:],
                            dram["sp_xc"][m, :, 512 * n:512 * (n + 1)],
                            waits=[(b.sem["XCS"], b.sem["XCS"].n),
                                   (VE, xct_free[bufz])],
                            inc=XT[bufz])
                yf1 = sop(lambda s, m=m, n=n: s.activation(
                    yfa[:, 512 * m:512 * (m + 1)],
                    arena[:, 2048 * m + 512 * n:2048 * m + 512 * (n + 1)],
                    AF.Identity, scale=scol[:, m:m + 1], bias=bcol[:, m:m + 1]),
                    waits=[(VE, gs3), (PE, yfa_free_pe), (SE, sq_se)])
                zs_se = sop(lambda s, bufz=bufz: s.activation(
                    zs[:], zt[bufz][:], AF.Silu),
                    waits=[(ZL[bufz], zld), (SE, yf1), (VE, zs_free[0])])
                zt_free[bufz] = zs_se
                yf2 = vop(lambda v, m=m: v.tensor_tensor(
                    yfa[:, 512 * m:512 * (m + 1)], yfa[:, 512 * m:512 * (m + 1)],
                    zs[:], ALU.mult), waits=[(SE, zs_se)])
                zs_free[0] = yf2
                yf3 = vop(lambda v, m=m, bufz=bufz: v.scalar_tensor_tensor(
                    yfa[:, 512 * m:512 * (m + 1)], xct[bufz][:],
                    w_dpar[:, m:m + 1], yfa[:, 512 * m:512 * (m + 1)],
                    ALU.mult, ALU.add), waits=[(XT[bufz], xcl), (VE, yf2)])
                xct_free[bufz] = yf3
            yfa_ve = VE.n
            for tt in range(4):
                ti = 4 * n + tt
                bufx = ti % 2
                xnl = b.dma("sync", sxn[bufx][:],
                            dram["sp_xn"][128 * ti:128 * (ti + 1), :],
                            waits=[(b.sem["XNS"], b.sem["XNS"].n),
                                   (VE, sxn_free[bufx])],
                            inc=XN4[bufx])
                for q in range(2):
                    bank = q
                    w = bank_write(bank, [(VE, yfa_ve), (OW, ow_tot)])
                    for r in range(NKI):
                        def omm(t, r=r, q=q, tt=tt, bank=bank):
                            return t.matmul(
                                pb[bank][:, :],
                                yfa[:, 512 * r + 128 * tt:512 * r + 128 * (tt + 1)],
                                arena[:, 32768 + 1024 * r + 512 * q:
                                      32768 + 1024 * r + 512 * (q + 1)],
                                start=(r == 0), stop=(r == NKI - 1))
                        b.op("tensor", omm, waits=w if r == 0 else [],
                             inc=PE if r == NKI - 1 else None)
                    bank_written(bank)
                    so_ = vop(lambda v, q=q, bufx=bufx, bank=bank: v.tensor_tensor(
                        sout[:, 512 * q:512 * (q + 1)], pb[bank][:, :],
                        sxn[bufx][:, 512 * q:512 * (q + 1)], ALU.add),
                        waits=bank_read(bank) + [(XN4[bufx], xnl),
                                                 (SST, sout_st)])
                    bank_did_read(bank, VE, so_)
                sxn_free[bufx] = so_
                sout_st = b.dma("scalar", dram["sp_s"][128 * ti:128 * (ti + 1), :],
                                sout[:], waits=[(VE, so_)], inc=SST)
                if debug:
                    sout_st = b.dma("scalar", dram["dbg_s"][128 * ti:128 * (ti + 1), :],
                                    sout[:], inc=SST)
            yfa_free_pe = PE.n

        env["p4_end"] = dict(pe=PE.n, ve=VE.n, se=SE.n)


def emit_p3(nc, b, es, dram, debug, env):
    b.barrier()
    """FFN (two passes with y1 spill) + final combine out = x + s*silu(xn+0.5*fc2)."""
    PE, VE, SE, GE = (b.sem[k] for k in ("PE", "VE", "SE", "GE"))
    pb = env["pb"]; arena = env["arena"]; cI = env["cI"]; cIb = env["cIb"]
    cOnes = env["cOnes"]; rstd2c = env["rstd2c"]
    w_g = None
    p4 = env["p4_end"]
    OUT = b.sem["OUT"]
    DBG = b.ctr("DBG")

    with contextlib.ExitStack() as p3s:
        def sb3(name, shape, dtype=F32):
            return p3s.enter_context(nc.sbuf_tensor(name, list(shape), dtype))

        S3 = b.ctr("S3")
        FQ = b.ctr("FQ")
        XN3 = [b.ctr("XN30"), b.ctr("XN31")]
        Y1L = [b.ctr("Y1L0"), b.ctr("Y1L1")]
        Y1G = [b.ctr("Y1G0"), b.ctr("Y1G1")]
        SL3 = [b.ctr("SL30"), b.ctr("SL31")]
        XL3 = [b.ctr("XL30"), b.ctr("XL31")]
        Y1S = b.ctr("Y1S")

        cSel8 = sb3("cSel8", (8, 1024))
        cSel32 = sb3("cSel32", (32, 4096))
        fs1 = sb3("fs1", (8, 4096))
        fs2 = sb3("fs2", (32, 1024))
        fb1 = sb3("fb1", (128, 32))
        wgc = sb3("wgc", (128, NKD))
        wb2 = sb3("wb2", (128, NKD), BF16)
        fb2r = sb3("fb2r", (1, DM))
        b.dma("sync", cSel8[:], dram["c_sel8"], inc=S3)
        b.dma("sync", cSel32[:], dram["c_sel32"], inc=S3)
        b.dma("sync", fs1[:], dram["fc1_s"], inc=S3)
        b.dma("sync", fs2[:], dram["fc2_s"], inc=S3)
        b.dma("sync", fb1[:], dram["fc1_b_c"], inc=S3)
        b.dma("sync", wgc[:], dram["ln_g_c"], inc=S3)
        b.dma("gpsimd", wb2[:], dram["ln_b_c"], inc=b.ctr("G3"))
        b.dma("sync", fb2r[:], dram["fc2_b_r"], inc=S3)
        s3tot = S3.n

        QT = [32768 + 4096, 32768 + 8192]
        Y1B = 32768
        u2b = sb3("u2b", (128, DM), BF16)
        u2T = sb3("u2T", (128, NKD * 256), BF16)
        xn3 = [sb3("xn30", (128, DM)), sb3("xn31", (128, DM))]
        y1t = sb3("y1t", (128, 256), BF16)
        y1ch0 = sb3("y1ch0", (128, 32 * 128), BF16)

        def y1sl(bufy, a, bb):
            if bufy == 0:
                return y1ch0[:, a:bb]
            return arena[:, Y1B + a:Y1B + bb]
        gts = sb3("gts", (128, 512))
        sld = [sb3("sld0", (128, DM))]
        xld3 = [sb3("xld30", (128, DM))]
        b1c = sb3("b1c", (128, 32))

        pb_w = [p4["pe"]] * 8
        pb_r = [[] for _ in range(8)]

        def bank_write(i, waits):
            w = list(waits) + pb_r[i]
            pb_r[i].clear()
            return w

        def bank_written(i):
            pb_w[i] = PE.n

        def bank_read(i):
            return [(PE, pb_w[i])]

        def bank_did_read(i, ctr, val):
            pb_r[i].append((ctr, val))

        def vop(fn, waits=()):
            return b.op("vector", fn, waits=waits, inc=VE)

        def sop(fn, waits=()):
            return b.op("scalar", fn, waits=waits, inc=SE)

        def pmm(bank_ap, lhsT, rhs, waits=(), start=True, stop=True):
            return b.op("tensor", lambda t: t.matmul(bank_ap, lhsT, rhs,
                                                     start=start, stop=stop),
                        waits=waits, inc=PE)

        deq_ve = 0
        deq_ve2b = [0, 0]
        for r in range(NKD):
            buf = r % 2
            fql = b.dma("gpsimd", arena[:, QT[buf]:QT[buf] + 4096],
                        dram["fc1_q"][128 * r:128 * (r + 1), :],
                        waits=[(PE, p4["pe"]), (SE, p4["se"]), (VE, p4["ve"])]
                        if r < 2 else [(VE, deq_ve2b[buf])],
                        inc=FQ)
            for sblk in range(8):
                bank = sblk % 2
                p_b = pmm(pb[bank][:, :], cSel8[:, 128 * r:128 * (r + 1)],
                          fs1[:, 512 * sblk:512 * (sblk + 1)],
                          waits=bank_write(bank, [(S3, s3tot), (FQ, fql)]))
                bank_written(bank)
                dv = vop(lambda v, r=r, sblk=sblk, bank=bank, buf=buf:
                         v.scalar_tensor_tensor(
                    arena[:, 4096 * r + 512 * sblk:4096 * r + 512 * (sblk + 1)],
                    arena[:, QT[buf] + 512 * sblk:QT[buf] + 512 * (sblk + 1)],
                    -128.0, pb[bank][:, :], ALU.add, ALU.mult),
                    waits=bank_read(bank) + ([(SE, p4["se"]), (VE, p4["ve"])]
                                             if r == 0 and sblk == 0 else []))
                bank_did_read(bank, VE, dv)
                deq_ve = dv
            for m in range(32):
                def bmm(t, m=m, r=r):
                    return t.matmul(pb[2][:, m:m + 1],
                                    arena[:, 4096 * r + 128 * m:4096 * r + 128 * (m + 1)],
                                    wb2[:, r:r + 1], start=True, stop=True)
                b.op("tensor", bmm,
                     waits=(bank_write(2, [(VE, deq_ve), (b.ctr("G3"), 16)])
                            if m == 0 else []),
                     inc=PE if m == 31 else None)
            bank_written(2)
            b1mm_pe = PE.n
            if r == 0:
                b1acc_ve = vop(lambda v: v.tensor_copy(b1c[:], pb[2][:, 0:32]),
                               waits=bank_read(2))
            else:
                b1acc_ve = vop(lambda v: v.tensor_tensor(b1c[:], b1c[:],
                                                         pb[2][:, 0:32], ALU.add),
                               waits=bank_read(2) + [(VE, b1acc_ve)])
            bank_did_read(2, VE, b1acc_ve)
            deq_ve = vop(lambda v, r=r: v.tensor_scalar(
                arena[:, 4096 * r:4096 * (r + 1)],
                arena[:, 4096 * r:4096 * (r + 1)], wgc[:, r:r + 1], None,
                ALU.mult), waits=[(VE, deq_ve), (S3, s3tot), (PE, b1mm_pe)])
            deq_ve2b[r % 2] = deq_ve
        deq1_ve = deq_ve
        bv = vop(lambda v: v.tensor_tensor(b1c[:], b1c[:], fb1[:], ALU.add),
                 waits=[(VE, b1acc_ve), (S3, s3tot)])

        if debug:
            DBG3 = b.ctr("DBG")
            b.dma("scalar", dram["dbg3_b1c"], b1c[:], waits=[(VE, bv)], inc=DBG3)
            b.dma("scalar", dram["dbg3_w1"], arena[:, 0:4096],
                  waits=[(VE, deq1_ve)], inc=DBG3)
            b.dma("scalar", dram["dbg3_rstd2"], rstd2c[:], inc=DBG3)

        u2T_free_pe = p4["pe"]
        u2b_free_pe = p4["pe"]
        xn_free = [0, 0]
        y1_st = 0
        tr_ve_prev = 0
        for g in range(8):
            u2_se = {}
            for u in range(2):
                ti = 2 * g + u
                bufx = ti % 2
                xnl = b.dma("sync", xn3[bufx][:],
                            dram["sp_xn"][128 * ti:128 * (ti + 1), :],
                            waits=[(b.sem["XNS"], b.sem["XNS"].n),
                                   (SE, xn_free[bufx])], inc=XN3[bufx])
                u2s = sop(lambda s, bufx=bufx, ti=ti: s.activation(
                    u2b[:], xn3[bufx][:], AF.Identity,
                    scale=rstd2c[:, ti:ti + 1]),
                    waits=[(XN3[bufx], xnl),
                           (PE, u2T_free_pe if u == 0 else u2b_free_pe),
                           (SE, p4["se"] if g == 0 and u == 0 else 0)])
                xn_free[bufx] = u2s
                for r in range(NKD):
                    bank = 2 + r % 2
                    p_t = pmm(pb[bank][:, 0:128], u2b[:, 128 * r:128 * (r + 1)],
                              cIb[:],
                              waits=bank_write(bank, [(SE, u2s)]))
                    bank_written(bank)
                    tr_ve_prev = vop(lambda v, r=r, u=u, bank=bank: v.tensor_copy(
                        u2T[:, 256 * r + 128 * u:256 * r + 128 * (u + 1)],
                        pb[bank][0:128, 0:128]), waits=bank_read(bank))
                    bank_did_read(bank, VE, tr_ve_prev)
                u2b_free_pe = PE.n
                u2_se[u] = tr_ve_prev
            u2T_ve = tr_ve_prev
            for m in range(32):
                bank = 4 + m % 2
                w = bank_write(bank, [(VE, u2T_ve), (VE, bv)])
                for r in range(NKD):
                    def fmm(t, m=m, r=r, bank=bank):
                        return t.matmul(pb[bank][:, 0:256],
                                        arena[:, 4096 * r + 128 * m:
                                              4096 * r + 128 * (m + 1)],
                                        u2T[:, 256 * r:256 * (r + 1)],
                                        start=(r == 0), stop=(r == NKD - 1))
                    b.op("tensor", fmm, waits=w if r == 0 else [],
                         inc=PE if r == NKD - 1 else None)
                bank_written(bank)
                sv = sop(lambda s, m=m, bank=bank: s.activation(
                    y1t[:], pb[bank][:, 0:256], AF.Silu, bias=b1c[:, m:m + 1]),
                    waits=bank_read(bank) + [(Y1S, y1_st)])
                bank_did_read(bank, SE, sv)
                y1_st = b.dma("scalar", dram["sp_y1"][m, :, 256 * g:256 * (g + 1)],
                              y1t[:], waits=[(SE, sv)], inc=Y1S)
            if debug and g == 0:
                DBG3b = b.ctr("DBG")
                b.dma("scalar", dram["dbg3_u2T"], u2T[:], waits=[(VE, u2T_ve)],
                      inc=DBG3b)
                b.wait("tensor", DBG3b, DBG3b.n)
            u2T_free_pe = PE.n

        passA_pe = PE.n
        deq_ve2 = 0
        deq_fb = [0, 0]
        for r in range(32):
            buf = r % 2
            fql = b.dma("gpsimd", arena[:, QT[buf]:QT[buf] + 1024],
                        dram["fc2_q"][128 * r:128 * (r + 1), :],
                        waits=[(PE, passA_pe)] if r < 2 else [(VE, deq_fb[buf])],
                        inc=FQ)
            for sblk in range(2):
                bank = sblk % 2
                p_b = pmm(pb[bank][:, :], cSel32[:, 128 * r:128 * (r + 1)],
                          fs2[:, 512 * sblk:512 * (sblk + 1)],
                          waits=bank_write(bank, [(FQ, fql)]))
                bank_written(bank)
                dv = vop(lambda v, r=r, sblk=sblk, bank=bank, buf=buf:
                         v.scalar_tensor_tensor(
                    arena[:, 1024 * r + 512 * sblk:1024 * r + 512 * (sblk + 1)],
                    arena[:, QT[buf] + 512 * sblk:QT[buf] + 512 * (sblk + 1)],
                    -128.0, pb[bank][:, :], ALU.add, ALU.mult),
                    waits=bank_read(bank) + ([(PE, passA_pe)]
                                             if r == 0 and sblk == 0 else []))
                bank_did_read(bank, VE, dv)
                deq_ve2 = dv
                deq_fb[buf] = dv

        y1_free_pe = [0, 0]
        s_free = [0, 0]
        x_free = [0, 0]
        ot_st = 0
        gts_se = 0
        for ti in range(NT):
            bufy = ti % 2
            yw = [(Y1S, b.sem["Y1S"].n), (PE, y1_free_pe[bufy])]
            b.dma("sync",
                  y1sl(bufy, 0, 2048).rearrange("p (a t) -> p a t", t=128),
                  dram["sp_y1"][0:16, :, 128 * ti:128 * (ti + 1)].rearrange(
                      "a p t -> p a t"),
                  waits=yw, inc=Y1L[bufy])
            y1g = b.dma("gpsimd",
                        y1sl(bufy, 2048, 4096).rearrange("p (a t) -> p a t", t=128),
                        dram["sp_y1"][16:32, :, 128 * ti:128 * (ti + 1)].rearrange(
                            "a p t -> p a t"),
                        waits=yw, inc=Y1G[bufy])
            y1l = Y1L[bufy].n
            xnl = b.dma("sync", xn3[ti % 2][:],
                        dram["sp_xn"][128 * ti:128 * (ti + 1), :],
                        waits=[(SE, xn_free[ti % 2])], inc=XN3[ti % 2])
            sl = b.dma("scalar", sld[0][:],
                       dram["sp_s"][128 * ti:128 * (ti + 1), :],
                       waits=[(b.sem["SST"], b.sem["SST"].n), (VE, s_free[0]),
                              (OUT, ot_st)],
                       inc=SL3[0])
            xl = b.dma("scalar", xld3[0][:],
                       dram["x"][128 * ti:128 * (ti + 1), :],
                       waits=[(VE, x_free[0])], inc=XL3[0])
            for q in range(2):
                bank = 4 + q
                w = bank_write(bank, [(Y1L[bufy], y1l), (Y1G[bufy], y1g),
                                      (VE, deq_ve2)])
                for r in range(32):
                    def f2mm(t, r=r, q=q, bank=bank, bufy=bufy):
                        return t.matmul(pb[bank][:, :],
                                        y1sl(bufy, 128 * r, 128 * (r + 1)),
                                        arena[:, 1024 * r + 512 * q:
                                              1024 * r + 512 * (q + 1)],
                                        start=(r == 0), stop=False)
                    b.op("tensor", f2mm, waits=w if r == 0 else [], inc=None)
                def fb2mm(t, q=q, bank=bank):
                    return t.matmul(pb[bank][:, :], cOnes[0:1, 0:128],
                                    fb2r[:, 512 * q:512 * (q + 1)],
                                    start=False, stop=True)
                b.op("tensor", fb2mm, waits=[(S3, s3tot)] if ti == 0 and q == 0 else [],
                     inc=PE)
                bank_written(bank)
                gv = vop(lambda v, q=q, ti=ti, bank=bank: v.scalar_tensor_tensor(
                    gts[:], pb[bank][:, :], 0.5, xn3[ti % 2][:, 512 * q:512 * (q + 1)],
                    ALU.mult, ALU.add),
                    waits=bank_read(bank) + [(XN3[ti % 2], xnl), (SE, gts_se)])
                bank_did_read(bank, VE, gv)
                gs = sop(lambda s: s.activation(gts[:], gts[:], AF.Silu),
                         waits=[(VE, gv)])
                gts_se = gs
                ov = vop(lambda v, q=q, ti=ti: v.tensor_tensor(
                    sld[0][:, 512 * q:512 * (q + 1)],
                    sld[0][:, 512 * q:512 * (q + 1)], gts[:], ALU.mult),
                    waits=[(SE, gs), (SL3[0], sl), (OUT, ot_st)])
                ov = vop(lambda v, q=q, ti=ti: v.tensor_tensor(
                    sld[0][:, 512 * q:512 * (q + 1)],
                    sld[0][:, 512 * q:512 * (q + 1)],
                    xld3[0][:, 512 * q:512 * (q + 1)], ALU.add),
                    waits=[(XL3[0], xl), (VE, ov)])
                if debug:
                    b.dma("scalar", dram["dbg_g"][128 * ti:128 * (ti + 1),
                                                  512 * q:512 * (q + 1)],
                          gts[:], waits=[(SE, gs)], inc=DBG)
            xn_free[ti % 2] = b.sem["SE"].n
            s_free[0] = ov
            x_free[0] = ov
            y1_free_pe[bufy] = PE.n
            ot_st = b.dma("scalar", dram["out"][128 * ti:128 * (ti + 1), :],
                          sld[0][:], waits=[(VE, ov)], inc=OUT)

def _relayout_cols(v, p=128):
    v = np.asarray(v).reshape(-1)
    return np.ascontiguousarray(v.reshape(-1, p).T)


def kernel(**inputs):
    consts = make_consts()
    nc = build_nc(debug=DEBUG)
    x = np.ascontiguousarray(inputs["L0_attn_x"])
    gw = np.asarray(inputs["L0_attn_gate_w"])
    base = dict(
        inw=np.ascontiguousarray(inputs["L0_attn_in_proj_w"]),
        q_w=np.ascontiguousarray(inputs["L0_attn_Q_w"]),
        readout_w=np.ascontiguousarray(inputs["L0_attn_readout_w"]),
        out_w=np.ascontiguousarray(inputs["L0_attn_out_w"]),
        fc1_q=np.ascontiguousarray(inputs["L0_ffn_fc1_q"]),
        fc2_q=np.ascontiguousarray(inputs["L0_ffn_fc2_q"]),
        fc1_s=np.ascontiguousarray(inputs["L0_ffn_fc1_scales"]),
        fc2_s=np.ascontiguousarray(inputs["L0_ffn_fc2_scales"]),
        gate_w_t=np.ascontiguousarray(
            gw.reshape(NKI, 128, 112).transpose(1, 0, 2).reshape(128, NKI * 112)),
        rms_w_c=_relayout_cols(inputs["L0_attn_rms_w"]),
        inb_zx_c=_relayout_cols(np.asarray(inputs["L0_attn_in_proj_b"])[:4096]),
        inb_kv_r=np.ascontiguousarray(
            np.asarray(inputs["L0_attn_in_proj_b"])[4096:].reshape(1, -1)),
        conv_w_c=np.ascontiguousarray(
            np.asarray(inputs["L0_attn_conv_w"]).reshape(NKI, 128, 4)
            .transpose(1, 0, 2).reshape(128, NKI * 4)),
        conv_b_c=_relayout_cols(inputs["L0_attn_conv_b"]),
        gate_b_r=np.ascontiguousarray(np.asarray(inputs["L0_attn_gate_b"]).reshape(1, -1)),
        lds_r=np.ascontiguousarray(np.asarray(inputs["L0_attn_log_dt_scale"]).reshape(1, -1)),
        gn_w_c=_relayout_cols(inputs["L0_attn_gn_w"]),
        gn_b_c=_relayout_cols(inputs["L0_attn_gn_b"]),
        d_param_c=_relayout_cols(inputs["L0_attn_D_param"]),
        ln_g_c=_relayout_cols(inputs["L0_ffn_ln_gamma"]),
        ln_b_c=_relayout_cols(inputs["L0_ffn_ln_beta"]),
        fc1_b_c=_relayout_cols(inputs["L0_ffn_fc1_bias"]),
        fc2_b_r=np.ascontiguousarray(np.asarray(inputs["L0_ffn_fc2_bias"]).reshape(1, -1)),
        **consts,
    )
    in_maps = []
    for c in range(8):
        m = dict(base)
        m["x"] = np.ascontiguousarray(x[c])
        in_maps.append(m)
    res = run_bass_kernel_spmd(nc, in_maps, core_ids=list(range(8)))
    out = np.stack([np.asarray(r["out"]) for r in res.results], axis=0)
    if DEBUG:
        kernel.debug_results = res.results
    return out.astype(np.float32)
